# version 12 (final)
# speedup vs baseline: 1.1562x; 1.0890x over previous
from contextlib import ExitStack

import numpy as np
import ml_dtypes
import concourse.bass as bass
import concourse.mybir as mybir
from concourse.bass_utils import run_bass_kernel_spmd

F32 = mybir.dt.float32
BF16 = mybir.dt.bfloat16
F16 = mybir.dt.float16
I32 = mybir.dt.int32
U16 = mybir.dt.uint16
AF = mybir.ActivationFunctionType
ALU = mybir.AluOpType
AX = mybir.AxisListType
bfdt = ml_dtypes.bfloat16

S = 2048
D = 2048
NT = S // 128
NF = D // 128
NEG = -1.0e30
FUSED = True


def host_consts():
    c = {}
    c["ident"] = np.eye(128, dtype=bfdt)
    R = np.zeros((64, 64), dtype=np.float32)
    for i in range(32):
        R[i, i + 32] = -1.0
        R[i + 32, i] = 1.0
    R2 = np.zeros((128, 128), dtype=np.float32)
    R2[:64, :64] = R
    R2[64:, 64:] = R
    c["r2t"] = np.ascontiguousarray(R2.T).astype(bfdt)
    inv = 1.0 / (10000.0 ** (np.arange(0, 64, 2, dtype=np.float32) / 64.0))
    ang = np.arange(S, dtype=np.float32)[:, None] * inv[None, :]
    ang = np.concatenate([ang, ang], axis=-1)
    cosT = np.cos(ang).T.astype(np.float32)
    sinT = np.sin(ang).T.astype(np.float32)
    c["cos2"] = np.ascontiguousarray(np.concatenate([cosT, cosT], 0))
    c["sin2"] = np.ascontiguousarray(np.concatenate([sinT, sinT], 0))
    m = np.zeros((128, 128), dtype=np.float32)
    iu = np.triu_indices(128, 1)
    m[iu] = NEG
    c["tril"] = m
    c["ones1"] = np.ones((1, 128), dtype=bfdt)
    return c


def asem(nc, name):
    st = nc.__dict__.setdefault("_semstate", {"free": [], "used": []})
    if st["free"]:
        h = st["free"].pop()
    else:
        h = nc.alloc_semaphore(name=name)
    st["used"].append(h)
    return h


def adsem(nc, name):
    return nc.alloc_semaphore(name=name)


def _reset_sems(nc):
    st = nc.__dict__.setdefault("_semstate", {"free": [], "used": []})
    if st["used"]:
        hs = list(st["used"])
        with nc.Block() as blk:
            @blk.sync
            def _(sync):
                for h in hs:
                    sync.sem_clear(h)
        st["free"].extend(hs)
        st["used"] = []


PRIME_F32 = [-1.0, 1.0 / 2048.0, float(1.0 / np.sqrt(192.0)), 0.125, -2.0,
             2.0, -0.5, 0.5, 1.0, -1e30]
PRIME_U16 = [15, 4, 12, 8, 7, 6, 0x8000, 0x3F00, 0x1C0, 255]
PRIME_I32 = [15, 4, 8, 12, 255]


def prime_consts(nc):
    scf = nc.sbuf_tensor("prime_f", [128, 1], F32).__enter__()
    scu = nc.sbuf_tensor("prime_u", [128, 1], U16).__enter__()
    sci = nc.sbuf_tensor("prime_i", [128, 1], I32).__enter__()
    pcs = nc.semaphore("prime_sem").__enter__()
    with nc.Block() as blk:
        @blk.vector
        def _(vector):
            n = 0
            def step(inst):
                nonlocal n
                inst.then_inc(pcs, 1)
                n += 1
                vector.wait_ge(pcs, n)
            step(vector.memset(scf[:, :], 0.0))
            step(vector.memset(scu[:, :], 0))
            step(vector.memset(sci[:, :], 0))
            for v in PRIME_F32:
                step(vector.tensor_scalar(scf[:, :], scf[:, :], v, None, ALU.mult))
            for v in PRIME_U16:
                step(vector.tensor_scalar(scu[:, :], scu[:, :], v, None, ALU.bitwise_and))
            for v in PRIME_I32:
                step(vector.tensor_scalar(sci[:, :], sci[:, :], v, None, ALU.bitwise_and))



def emit_prep(nc, x_d, xnT_d, ident_d, tag="p"):
    _reset_sems(nc)
    with ExitStack() as ctx:
        e = ctx.enter_context
        xt = e(nc.sbuf_tensor(tag + "_x", [128, 4, D], F32))
        sq = e(nc.sbuf_tensor(tag + "_sq", [128, D], F32))
        xn = e(nc.sbuf_tensor(tag + "_xn", [128, 2, D], BF16))
        st = e(nc.sbuf_tensor(tag + "_st", [128, 12], F32))
        ident = e(nc.sbuf_tensor(tag + "_id", [128, 128], BF16))
        xnT = e(nc.sbuf_tensor(tag + "_out", [128, NF, S], BF16))
        psT = e(nc.psum_tensor(tag + "_ps", [128, 2, 8, 128], BF16))
        lid = asem(nc, tag + "_li")
        ld = [asem(nc, tag + "_l%d" % i) for i in range(4)]
        vc = asem(nc, tag + "_vc")
        sq_done = asem(nc, tag + "_sq")
        sv = asem(nc, tag + "_sv")
        sl = asem(nc, tag + "_sl")
        sa = asem(nc, tag + "_sa")
        sx = asem(nc, tag + "_sx")
        mm = asem(nc, tag + "_mm")
        ev = asem(nc, tag + "_ev")
        so = adsem(nc, tag + "_so")
        blk = e(nc.Block())
        @blk.sync
        def _(sync):
            sync.dma_start(out=ident[:, :], in_=ident_d[:, :]).then_inc(lid, 16)
            for t in range(NT):
                if t >= 4:
                    sync.wait_ge(sx, t - 3)
                sync.dma_start(out=xt[:, t % 4, :],
                               in_=x_d[128 * t:128 * (t + 1), :]).then_inc(ld[t % 4], 16)

        @blk.vector
        def _(vector):
            nvc = 0
            def step(inst):
                nonlocal nvc
                inst.then_inc(vc, 1)
                nvc += 1
                vector.wait_ge(vc, nvc)
            for t in range(NT):
                r = t % 4
                vector.wait_ge(ld[r], 16 * (t // 4 + 1))
                step(vector.tensor_reduce(st[:, 0:1], xt[:, r, :], AX.X, ALU.add))
                step(vector.tensor_scalar(st[:, 2:3], st[:, 0:1], 1.0 / D, None, ALU.mult))
                step(vector.tensor_tensor(st[:, 3:4], st[:, 2:3], st[:, 2:3], ALU.mult))
                vector.wait_ge(sq_done, t + 1)
                step(vector.tensor_scalar(st[:, 4:5], st[:, 1:2], 1.0 / D, None, ALU.mult))
                vector.tensor_tensor(st[:, 4:5], st[:, 4:5], st[:, 3:4],
                                     ALU.subtract).then_inc(sv, 1)
                vector.wait_ge(sa, t + 1)
                vector.tensor_scalar(xn[:, t % 2, :], xt[:, r, :], st[:, 2:3],
                                     st[:, 5:6], ALU.subtract,
                                     ALU.mult).then_inc(sx, 1)
                vector.wait_ge(mm, 2 * t + 1)
                vector.tensor_copy(xnT[:, 0:8, 128 * t:128 * (t + 1)],
                                   psT[:, 0, :, :])
                vector.wait_ge(mm, 2 * t + 2)
                vector.tensor_copy(xnT[:, 8:16, 128 * t:128 * (t + 1)],
                                   psT[:, 1, :, :]).then_inc(ev, 2)

        @blk.scalar
        def _(scalar):
            for t in range(NT):
                r = t % 4
                scalar.wait_ge(ld[r], 16 * (t // 4 + 1))
                scalar.activation(sq[:, :], xt[:, r, :], AF.Square,
                                  accum_out=st[:, 1:2]).then_inc(sq_done, 1)
                scalar.wait_ge(sv, t + 1)
                scalar.activation(st[:, 6:7], st[:, 4:5], AF.Ln,
                                  bias=np.float32(1e-5), scale=1.0).then_inc(sl, 1)
                scalar.wait_ge(sl, t + 1)
                scalar.activation(st[:, 5:6], st[:, 6:7], AF.Exp, bias=0.0,
                                  scale=np.float32(-0.5)).then_inc(sa, 1)

        @blk.tensor
        def _(tensor):
            tensor.wait_ge(lid, 16)
            for t in range(NT):
                tensor.wait_ge(sx, t + 1)
                if t >= 1:
                    tensor.wait_ge(ev, 2 * t)
                for half in range(2):
                    for f in range(8):
                        fc = 8 * half + f
                        inst = tensor.transpose(psT[:, half, f, :],
                                                xn[:, t % 2, 128 * fc:128 * (fc + 1)],
                                                ident[:, :])
                    inst.then_inc(mm, 1)

        @blk.gpsimd
        def _(g):
            g.wait_ge(ev, 2 * NT)
            for f in range(NF):
                g.dma_start(out=xnT_d[128 * f:128 * (f + 1), :],
                            in_=xnT[:, f, :]).then_inc(so, 16)
            g.wait_ge(so, 16 * NF)
    return nc



def emit_attn0(nc, T, tag="a0"):
    _reset_sems(nc)
    H = 16
    CQ = 1536
    CKV = 512
    NKQ = CQ // 128
    NKV = CKV // 128
    NS = S // 512
    NG = NT // 4
    NIDX0 = 16 * NT
    with ExitStack() as octx:
        oe = octx.enter_context
        kvlT = oe(nc.sbuf_tensor(tag + "_kvlT", [128, NKV, S], BF16))
        kpeT = oe(nc.sbuf_tensor(tag + "_kpeT", [128, S], BF16))
        identS = oe(nc.sbuf_tensor(tag + "_id", [128, 128], BF16))
        ldc = asem(nc, tag + "_ldc")
        qlT_cm = nc.sbuf_tensor(tag + "_qlT", [128, NKQ, S], BF16)
        qlT = qlT_cm.__enter__()

        with ExitStack() as ctx:
            e = ctx.enter_context
            xnT = e(nc.sbuf_tensor(tag + "_xnT", [128, NF, S], BF16))
            wdq = e(nc.sbuf_tensor(tag + "_wdq", [128, NF, CQ], BF16))
            qns = e(nc.sbuf_tensor(tag + "_qns", [128, NKQ], F32))
            qlf = e(nc.sbuf_tensor(tag + "_qlf", [128, 2, CQ], F32))
            qlb = e(nc.sbuf_tensor(tag + "_qlb", [128, 2, CQ], BF16))
            sqs = e(nc.sbuf_tensor(tag + "_sqs", [128, CQ], BF16))
            st = e(nc.sbuf_tensor(tag + "_st", [128, 4], F32))
            ps = e(nc.psum_tensor(tag + "_ps", [128, 2, 3, 512], F32))
            psT = e(nc.psum_tensor(tag + "_psT", [128, NKQ, 128], BF16))
            lw = asem(nc, tag + "_lw")
            lwg = adsem(nc, tag + "_lwg")
            mmq = asem(nc, tag + "_mmq")
            evq = asem(nc, tag + "_evq")
            vsq = asem(nc, tag + "_vsq")
            ssq = asem(nc, tag + "_ssq")
            sqd = asem(nc, tag + "_sqd")
            sa = asem(nc, tag + "_sa")
            qb = asem(nc, tag + "_qb")
            mt = asem(nc, tag + "_mt")
            qe = asem(nc, tag + "_qe")
            blk = e(nc.Block())

            @blk.sync
            def _(sync):
                sync.dma_start(out=identS[:, :], in_=T["ident"][:, :]).then_inc(ldc, 16)
                for i in range(4):
                    sync.dma_start(
                        out=xnT[:, 4 * i:4 * (i + 1), :],
                        in_=T["xnT"].ap().rearrange("(c p) s -> p c s", p=128)[:, 4 * i:4 * (i + 1), :],
                    ).then_inc(lw, 16)
                sync.dma_start(out=qns[:, :],
                               in_=T["qnsT"][:, :]).then_inc(lw, 16)

            @blk.gpsimd
            def _(g):
                g.dma_start(
                    out=wdq[:, :, :],
                    in_=T["w_dq"].ap().rearrange("(c p) f -> p c f", p=128),
                ).then_inc(lwg, 16)

            @blk.tensor
            def _(tensor):
                tensor.wait_ge(ldc, 16)
                tensor.wait_ge(lw, 16 * 5)
                tensor.wait_ge(lwg, 16)
                for t in range(NT + 1):
                    if t < NT:
                        if t >= 2:
                            tensor.wait_ge(evq, t - 1)
                        for n in range(3):
                            for k in range(NF):
                                inst = tensor.matmul(ps[:, t % 2, n, :],
                                                     xnT[:, k, 128 * t:128 * (t + 1)],
                                                     wdq[:, k, 512 * n:512 * (n + 1)],
                                                     start=(k == 0), stop=(k == NF - 1))
                        inst.then_inc(mmq, 1)
                    if t >= 1:
                        u = t - 1
                        tensor.wait_ge(qb, u + 1)
                        if u >= 1:
                            tensor.wait_ge(qe, u)
                        for c in range(NKQ):
                            inst = tensor.transpose(psT[:, c, :],
                                                    qlb[:, u % 2, 128 * c:128 * (c + 1)],
                                                    identS[:, :])
                        inst.then_inc(mt, 1)

            @blk.vector
            def _(vector):
                for t in range(NT):
                    vector.wait_ge(mmq, t + 1)
                    if t >= 2:
                        vector.wait_ge(ssq, t - 1)
                    for n in range(3):
                        inst = vector.tensor_copy(qlf[:, t % 2, 512 * n:512 * (n + 1)],
                                                  ps[:, t % 2, n, :])
                    inst.then_inc(evq, 1)
                    vector.wait_ge(sa, t + 1)
                    if t >= 2:
                        vector.wait_ge(mt, t - 1)
                    vector.tensor_scalar(qlb[:, t % 2, :], qlf[:, t % 2, :],
                                         st[:, 0:1], None, ALU.mult).then_inc(qb, 1)
                    if t >= 1:
                        vector.wait_ge(mt, t)
                        vector.tensor_copy(qlT[:, :, 128 * (t - 1):128 * t],
                                           psT[:, :, :]).then_inc(qe, 1)
                vector.wait_ge(mt, NT)
                vector.tensor_copy(qlT[:, :, 128 * (NT - 1):128 * NT],
                                   psT[:, :, :]).then_inc(qe, 1)

            @blk.scalar
            def _(scalar):
                for t in range(NT):
                    scalar.wait_ge(evq, t + 1)
                    scalar.activation(sqs[:, :], qlf[:, t % 2, :], AF.Square,
                                      accum_out=st[:, 1:2]).then_inc(ssq, 1)
                    scalar.wait_ge(ssq, t + 1)
                    scalar.activation(st[:, 2:3], st[:, 1:2], AF.Ln,
                                      bias=np.float32(1e-6),
                                      scale=np.float32(1.0 / CQ)).then_inc(sqd, 1)
                    scalar.wait_ge(sqd, t + 1)
                    scalar.activation(st[:, 0:1], st[:, 2:3], AF.Exp, bias=0.0,
                                      scale=np.float32(-0.5)).then_inc(sa, 1)

        with ExitStack() as ctx:
            e = ctx.enter_context
            xnT = e(nc.sbuf_tensor(tag + "_xnT2", [128, NF, S], BF16))
            wdkv = e(nc.sbuf_tensor(tag + "_wdkv", [128, NF, CKV], BF16))
            wkpe = e(nc.sbuf_tensor(tag + "_wkpe", [128, NF, 64], BF16))
            kvns = e(nc.sbuf_tensor(tag + "_kvns", [128, NKV], F32))
            cosN = e(nc.sbuf_tensor(tag + "_cosN", [128, NT, 64], F32))
            sinN = e(nc.sbuf_tensor(tag + "_sinN", [128, NT, 64], F32))
            kvf = e(nc.sbuf_tensor(tag + "_kvf", [128, 2, CKV], F32))
            kvb = e(nc.sbuf_tensor(tag + "_kvb", [128, 2, CKV], BF16))
            kpf = e(nc.sbuf_tensor(tag + "_kpf", [128, 2, 64], F32))
            rot = e(nc.sbuf_tensor(tag + "_rot", [128, 64], F32))
            kpb = e(nc.sbuf_tensor(tag + "_kpb", [128, 2, 64], BF16))
            sqs = e(nc.sbuf_tensor(tag + "_sqs2", [128, CKV], F32))
            st = e(nc.sbuf_tensor(tag + "_st2", [128, 4], F32))
            ps = e(nc.psum_tensor(tag + "_ps2", [128, 2, 2, 512], F32))
            psT = e(nc.psum_tensor(tag + "_psT2", [128, 5, 128], BF16))
            lw = asem(nc, tag + "_lw2")
            lwg = adsem(nc, tag + "_lwg2")
            mmq = asem(nc, tag + "_mmq2")
            evq = asem(nc, tag + "_evq2")
            vsq = asem(nc, tag + "_vsq2")
            ssq = asem(nc, tag + "_ssq2")
            sqd = asem(nc, tag + "_sqd2")
            sa = asem(nc, tag + "_sa2")
            qb = asem(nc, tag + "_qb2")
            vch = asem(nc, tag + "_vch2")
            mt = asem(nc, tag + "_mt2")
            qe = asem(nc, tag + "_qe2")
            blk = e(nc.Block())

            @blk.sync
            def _(sync):
                for i in range(4):
                    sync.dma_start(
                        out=xnT[:, 4 * i:4 * (i + 1), :],
                        in_=T["xnT"].ap().rearrange("(c p) s -> p c s", p=128)[:, 4 * i:4 * (i + 1), :],
                    ).then_inc(lw, 16)
                sync.dma_start(out=kvns[:, :], in_=T["kvnsT"][:, :]).then_inc(lw, 16)
                sync.dma_start(out=cosN[:, :, :],
                               in_=T["cosN"].ap().rearrange("(c p) f -> p c f", p=128)).then_inc(lw, 16)
                sync.dma_start(out=sinN[:, :, :],
                               in_=T["sinN"].ap().rearrange("(c p) f -> p c f", p=128)).then_inc(lw, 16)

            @blk.gpsimd
            def _(g):
                g.dma_start(out=wdkv[:, :, :],
                            in_=T["w_dkv"].ap().rearrange("(c p) f -> p c f", p=128)
                            ).then_inc(lwg, 16)
                g.dma_start(out=wkpe[:, :, :],
                            in_=T["w_kpe"].ap().rearrange("(c p) f -> p c f", p=128)
                            ).then_inc(lwg, 16)

            @blk.tensor
            def _(tensor):
                tensor.wait_ge(lw, 16 * 7)
                tensor.wait_ge(lwg, 32)
                for t in range(NT + 1):
                    if t < NT:
                        if t >= 2:
                            tensor.wait_ge(evq, t - 1)
                        for k in range(NF):
                            tensor.matmul(ps[:, t % 2, 0, :],
                                          xnT[:, k, 128 * t:128 * (t + 1)],
                                          wdkv[:, k, :],
                                          start=(k == 0), stop=(k == NF - 1))
                        for k in range(NF):
                            inst = tensor.matmul(ps[:, t % 2, 1, 0:64],
                                                 xnT[:, k, 128 * t:128 * (t + 1)],
                                                 wkpe[:, k, :],
                                                 start=(k == 0), stop=(k == NF - 1))
                        inst.then_inc(mmq, 1)
                    if t >= 1:
                        u = t - 1
                        tensor.wait_ge(qb, u + 1)
                        if u >= 1:
                            tensor.wait_ge(qe, u)
                        for c in range(NKV):
                            tensor.transpose(psT[:, c, :],
                                             kvb[:, u % 2, 128 * c:128 * (c + 1)],
                                             identS[:, :])
                        tensor.transpose(psT[0:64, 4, :], kpb[:, u % 2, :],
                                         identS[:, :])
                        inst = tensor.transpose(psT[64:128, 4, :], kpb[:, u % 2, :],
                                                identS[:, :])
                        inst.then_inc(mt, 1)

            @blk.vector
            def _(vector):
                nv = 0
                def step(inst):
                    nonlocal nv
                    inst.then_inc(vch, 1)
                    nv += 1
                    vector.wait_ge(vch, nv)
                vector.wait_ge(lw, 16 * 7)
                for t in range(NT):
                    vector.wait_ge(mmq, t + 1)
                    if t >= 2:
                        vector.wait_ge(ssq, t - 1)
                    vector.tensor_copy(kvf[:, t % 2, :], ps[:, t % 2, 0, :]).then_inc(vsq, 1)
                    vector.tensor_copy(kpf[:, t % 2, :], ps[:, t % 2, 1, 0:64]).then_inc(evq, 1)
                    vector.wait_ge(evq, t + 1)
                    if t >= 2:
                        vector.wait_ge(mt, t - 1)
                    step(vector.tensor_scalar(rot[:, 0:32], kpf[:, t % 2, 32:64],
                                              -1.0, None, ALU.mult))
                    step(vector.tensor_copy(rot[:, 32:64], kpf[:, t % 2, 0:32]))
                    step(vector.tensor_tensor(rot[:, :], rot[:, :],
                                              sinN[:, t, :], ALU.mult))
                    step(vector.tensor_tensor(kpf[:, t % 2, :], kpf[:, t % 2, :],
                                              cosN[:, t, :], ALU.mult))
                    step(vector.tensor_tensor(kpb[:, t % 2, :], kpf[:, t % 2, :],
                                              rot[:, :], ALU.add))
                    vector.wait_ge(sa, t + 1)
                    vector.tensor_scalar(kvb[:, t % 2, :], kvf[:, t % 2, :],
                                         st[:, 0:1], None, ALU.mult).then_inc(qb, 1)
                    if t >= 1:
                        vector.wait_ge(mt, t)
                        vector.tensor_copy(kvlT[:, :, 128 * (t - 1):128 * t],
                                           psT[:, 0:NKV, :])
                        vector.tensor_copy(kpeT[:, 128 * (t - 1):128 * t],
                                           psT[:, 4, :]).then_inc(qe, 1)
                vector.wait_ge(mt, NT)
                vector.tensor_copy(kvlT[:, :, 128 * (NT - 1):128 * NT],
                                   psT[:, 0:NKV, :])
                vector.tensor_copy(kpeT[:, 128 * (NT - 1):128 * NT],
                                   psT[:, 4, :]).then_inc(qe, 1)

            @blk.scalar
            def _(scalar):
                for t in range(NT):
                    scalar.wait_ge(vsq, t + 1)
                    scalar.activation(sqs[:, :], kvf[:, t % 2, :], AF.Square,
                                      accum_out=st[:, 1:2]).then_inc(ssq, 1)
                    scalar.wait_ge(ssq, t + 1)
                    scalar.activation(st[:, 2:3], st[:, 1:2], AF.Ln,
                                      bias=np.float32(1e-6),
                                      scale=np.float32(1.0 / CKV)).then_inc(sqd, 1)
                    scalar.wait_ge(sqd, t + 1)
                    scalar.activation(st[:, 0:1], st[:, 2:3], AF.Exp, bias=0.0,
                                      scale=np.float32(-0.5)).then_inc(sa, 1)

        qn_cm = nc.sbuf_tensor(tag + "_qn", [128, H, S], BF16, side="right")
        qT_nope = qn_cm.__enter__()
        qp_cm = nc.sbuf_tensor(tag + "_qp", [128, H // 2, S], BF16, side="right")
        qT_pe = qp_cm.__enter__()
        with ExitStack() as ctx:
            e = ctx.enter_context
            wnf = e(nc.sbuf_tensor(tag + "_wnf", [128, 1, NKQ, 128], F32))
            wpf = e(nc.sbuf_tensor(tag + "_wpf", [128, 1, NKQ, 128], F32))
            wnb = e(nc.sbuf_tensor(tag + "_wnb", [128, 1, NKQ, 128], BF16))
            wpb = e(nc.sbuf_tensor(tag + "_wpb", [128, 1, NKQ, 128], BF16))
            qns = e(nc.sbuf_tensor(tag + "_qns2", [128, NKQ], F32))
            r2t = e(nc.sbuf_tensor(tag + "_r2t", [128, 128], BF16))
            cos2 = e(nc.sbuf_tensor(tag + "_cos2", [128, S], BF16))
            sin2 = e(nc.sbuf_tensor(tag + "_sin2", [128, S], BF16))
            tmpA = e(nc.sbuf_tensor(tag + "_tmpA", [128, 512], F32))
            tmpB = e(nc.sbuf_tensor(tag + "_tmpB", [128, 512], F32))
            psA = e(nc.psum_tensor(tag + "_psA", [128, 4, 512], F32))
            psB = e(nc.psum_tensor(tag + "_psB", [128, 4, 512], F32))
            lw = asem(nc, tag + "_lw3")
            wns = asem(nc, tag + "_wns")
            wps = asem(nc, tag + "_wps")
            cvn = asem(nc, tag + "_cvn")
            cvp = asem(nc, tag + "_cvp")
            vch = asem(nc, tag + "_vchB2")
            mmn = asem(nc, tag + "_mmn")
            mme = asem(nc, tag + "_mme")
            mmr = asem(nc, tag + "_mmr")
            evn = asem(nc, tag + "_evn")
            evp = asem(nc, tag + "_evp")
            evr = asem(nc, tag + "_evr")
            blk = e(nc.Block())

            @blk.sync
            def _(sync):
                sync.dma_start(out=r2t[:, :], in_=T["r2t"][:, :]).then_inc(lw, 16)
                sync.dma_start(out=cos2[:, :], in_=T["cos2"][:, :]).then_inc(lw, 16)
                sync.dma_start(out=sin2[:, :], in_=T["sin2"][:, :]).then_inc(lw, 16)
                sync.dma_start(out=qns[:, :], in_=T["qnsT"][:, :]).then_inc(lw, 16)
                for j in range(H // 2):
                    for hh in range(2):
                        h = 2 * j + hh
                        if h >= 1:
                            sync.wait_ge(cvn, h)
                        sync.dma_start(
                            out=wnf[:, 0, :, :],
                            in_=T["w_uq"].ap().rearrange("(c p) f -> p c f", p=128)[:, :, 192 * h:192 * h + 128],
                        ).then_inc(wns, 16)
                    if j >= 1:
                        sync.wait_ge(cvp, j)
                    sync.dma_start(
                        out=wpf[:, 0, :, 0:64],
                        in_=T["w_uq"].ap().rearrange("(c p) f -> p c f", p=128)[:, :, 384 * j + 128:384 * j + 192],
                    ).then_inc(wps, 16)
                    sync.dma_start(
                        out=wpf[:, 0, :, 64:128],
                        in_=T["w_uq"].ap().rearrange("(c p) f -> p c f", p=128)[:, :, 384 * j + 320:384 * j + 384],
                    ).then_inc(wps, 16)

            @blk.vector
            def _(vector):
                nv = 0
                def step(inst):
                    nonlocal nv
                    inst.then_inc(vch, 1)
                    nv += 1
                    vector.wait_ge(vch, nv)
                vector.wait_ge(lw, 16 * 4)
                for j in range(H // 2):
                    for hh in range(2):
                        h = 2 * j + hh
                        vector.wait_ge(wns, 16 * (h + 1))
                        if h >= 1:
                            vector.wait_ge(mmn, h)
                        for c in range(NKQ):
                            inst = vector.tensor_scalar(wnb[:, 0, c, :],
                                                        wnf[:, 0, c, :],
                                                        qns[:, c:c + 1], None, ALU.mult)
                        inst.then_inc(cvn, 1)
                        vector.wait_ge(mmn, h + 1)
                        for n in range(NS):
                            inst = vector.tensor_copy(qT_nope[:, h, 512 * n:512 * (n + 1)],
                                                      psA[:, n, :])
                        inst.then_inc(evn, 1)
                    vector.wait_ge(wps, 32 * (j + 1))
                    if j >= 1:
                        vector.wait_ge(mme, j)
                    for c in range(NKQ):
                        inst = vector.tensor_scalar(wpb[:, 0, c, :],
                                                    wpf[:, 0, c, :],
                                                    qns[:, c:c + 1], None, ALU.mult)
                    inst.then_inc(cvp, 1)
                    vector.wait_ge(mme, j + 1)
                    for n in range(NS):
                        inst = vector.tensor_copy(qT_pe[:, j, 512 * n:512 * (n + 1)],
                                                  psB[:, n, :])
                    inst.then_inc(evp, 1)
                    vector.wait_ge(mmr, j + 1)
                    for n in range(NS):
                        sl_ = slice(512 * n, 512 * (n + 1))
                        step(vector.tensor_tensor(tmpA[:, :], qT_pe[:, j, sl_],
                                                  cos2[:, sl_], ALU.mult))
                        step(vector.tensor_tensor(tmpB[:, :], psA[:, n, :],
                                                  sin2[:, sl_], ALU.mult))
                        step(vector.tensor_tensor(qT_pe[:, j, sl_], tmpA[:, :],
                                                  tmpB[:, :], ALU.add))
                    vector.nop().then_inc(evr, 1)

            @blk.tensor
            def _(tensor):
                tensor.wait_ge(lw, 16 * 4)
                for j in range(H // 2):
                    for hh in range(2):
                        h = 2 * j + hh
                        tensor.wait_ge(cvn, h + 1)
                        if h >= 1:
                            tensor.wait_ge(evn, h)
                        if j >= 1 and hh == 0:
                            tensor.wait_ge(evr, j)
                        for n in range(NS):
                            for k in range(NKQ):
                                inst = tensor.matmul(psA[:, n, :], wnb[:, 0, k, :],
                                                     qlT[:, k, 512 * n:512 * (n + 1)],
                                                     start=(k == 0), stop=(k == NKQ - 1))
                        inst.then_inc(mmn, 1)
                    tensor.wait_ge(cvp, j + 1)
                    if j >= 1:
                        tensor.wait_ge(evp, j)
                    for hh in range(2):
                        h = 2 * j + hh
                        base = 64 * hh
                        for n in range(NS):
                            for k in range(NKQ):
                                inst = tensor.matmul(psB[base:base + 64, n, :],
                                                     wpb[:, 0, k, 64 * hh:64 * (hh + 1)],
                                                     qlT[:, k, 512 * n:512 * (n + 1)],
                                                     start=(k == 0), stop=(k == NKQ - 1))
                    inst.then_inc(mme, 1)
                    tensor.wait_ge(evp, j + 1)
                    tensor.wait_ge(evn, 2 * j + 2)
                    for n in range(NS):
                        inst = tensor.matmul(psA[:, n, :], r2t[:, :],
                                             qT_pe[:, j, 512 * n:512 * (n + 1)],
                                             start=True, stop=True)
                    inst.then_inc(mmr, 1)

        qlT_cm.__exit__(None, None, None)

        SC = float(1.0 / np.sqrt(192.0))
        with ExitStack() as ctx:
            e = ctx.enter_context
            wku = e(nc.sbuf_tensor(tag + "_wku", [128, 2, NKV, 256], F32))
            knb = e(nc.sbuf_tensor(tag + "_knb", [128, NKV, 128], BF16))
            vb = e(nc.sbuf_tensor(tag + "_vb", [128, NKV, 128], BF16))
            kvns = e(nc.sbuf_tensor(tag + "_kvns2", [128, NKV], F32))
            knT = e(nc.sbuf_tensor(tag + "_knT", [128, S], BF16))
            v_sb = e(nc.sbuf_tensor(tag + "_vsb", [128, NT, 128], BF16))
            srow = e(nc.sbuf_tensor(tag + "_srow", [128, 2, S], F32))
            probs = e(nc.sbuf_tensor(tag + "_probs", [128, 2, S], BF16))
            probs2 = e(nc.sbuf_tensor(tag + "_probs2", [128, 2, S], BF16))
            probsT = e(nc.sbuf_tensor(tag + "_probsT", [128, NT, 128], BF16))
            ahT = e(nc.sbuf_tensor(tag + "_ahT", [128, 2, S], BF16))
            tril = e(nc.sbuf_tensor(tag + "_tril", [128, 128], F32))
            stt = e(nc.sbuf_tensor(tag + "_stt", [128, 2, 4], F32))
            psA = e(nc.psum_tensor(tag + "_psA3", [128, 4, 512], F32))
            psB = e(nc.psum_tensor(tag + "_psB3", [128, 512], F32))
            psD = e(nc.psum_tensor(tag + "_psD3", [128, 8, 128], BF16))
            psE = e(nc.psum_tensor(tag + "_psE3", [128, 2, 512], F32))
            lw = asem(nc, tag + "_lw4")
            wkus = [asem(nc, tag + "_wku%d" % i) for i in range(2)]
            vch = asem(nc, tag + "_vch3")
            cvk = asem(nc, tag + "_cvk")
            mkn = asem(nc, tag + "_mkn")
            ekn = asem(nc, tag + "_ekn")
            mv = asem(nc, tag + "_mv")
            evv = asem(nc, tag + "_evv")
            mqk = asem(nc, tag + "_mqk")
            sev = asem(nc, tag + "_sev")
            svr = asem(nc, tag + "_svr")
            sex = asem(nc, tag + "_sex")
            spr = asem(nc, tag + "_spr")
            mtr = asem(nc, tag + "_mtr")
            etr = asem(nc, tag + "_etr")
            mav = asem(nc, tag + "_mav")
            eav = asem(nc, tag + "_eav")
            hdma = [adsem(nc, tag + "_hdma%d" % i) for i in range(2)]
            blk = e(nc.Block())

            def groups_of(qi):
                return (qi + 8) // 8

            gcum = [0]
            for h in range(H):
                for qi in range(NT):
                    gcum.append(gcum[-1] + groups_of(qi))

            @blk.sync
            def _(sync):
                sync.dma_start(out=tril[:, :], in_=T["tril"][:, :]).then_inc(lw, 16)
                sync.dma_start(out=kvns[:, :], in_=T["kvnsT"][:, :]).then_inc(lw, 16)
                for h in range(H):
                    if h >= 2:
                        sync.wait_ge(cvk, h - 1)
                    sync.dma_start(
                        out=wku[:, h % 2, :, :],
                        in_=T["w_ukv"].ap().rearrange("(c p) f -> p c f", p=128)[:, :, 256 * h:256 * (h + 1)],
                    ).then_inc(wkus[h % 2], 16)

            @blk.vector
            def _(vector):
                vector.wait_ge(lw, 32)
                nv = [0]
                def step(inst):
                    nv[0] += 1
                    inst.then_inc(vch, 1)
                    vector.wait_ge(vch, nv[0])

                def sm_front(h, qi):
                    idx = NT * h + qi
                    sl = idx % 2
                    kw = 128 * (qi + 1)
                    vector.wait_ge(mqk, idx + 1)
                    if idx >= 2:
                        vector.wait_ge(sex, idx - 1)
                    for kj in range(qi // 4):
                        vector.tensor_scalar(srow[:, sl, 512 * kj:512 * (kj + 1)],
                                             psA[:, kj, :], SC, None, ALU.mult)
                    rem0 = 512 * (qi // 4)
                    if kw - 128 > rem0:
                        vector.tensor_scalar(
                            srow[:, sl, rem0:kw - 128],
                            psA[:, qi // 4, 0:kw - 128 - rem0], SC, None, ALU.mult)
                    vector.tensor_scalar(
                        srow[:, sl, kw - 128:kw],
                        psA[:, qi // 4, 128 * (qi % 4):128 * (qi % 4 + 1)],
                        SC, None, ALU.mult).then_inc(sev, 1)
                    vector.wait_ge(sev, idx + 1)
                    step(vector.tensor_tensor(srow[:, sl, kw - 128:kw],
                                              srow[:, sl, kw - 128:kw],
                                              tril[:, :], ALU.add))
                    vector.tensor_reduce(stt[:, sl, 0:1], srow[:, sl, 0:kw], AX.X,
                                         ALU.max, negate=True).then_inc(svr, 1)

                def sm_back(u):
                    uh, uqi = u // NT, u % NT
                    sl = u % 2
                    kw = 128 * (uqi + 1)
                    vector.wait_ge(sex, u + 1)
                    step(vector.reciprocal(stt[:, sl, 2:3], stt[:, sl, 1:2]))
                    if u >= 2:
                        vector.wait_ge(mtr, gcum[u - 1])
                    vector.tensor_scalar(probs2[:, sl, 0:kw], probs[:, sl, 0:kw],
                                         stt[:, sl, 2:3], None,
                                         ALU.mult).then_inc(spr, 1)

                def av_evac(u):
                    vector.wait_ge(mav, u + 1)
                    uh, uqi = u // NT, u % NT
                    if uh >= 2 and uqi == 0:
                        vector.wait_ge(hdma[uh % 2], 16 * (uh // 2))
                    vector.tensor_copy(ahT[:, uh % 2, 128 * uqi:128 * (uqi + 1)],
                                       psE[:, u % 2, 0:128]).then_inc(eav, 1)

                prog = {"back": 0, "evac": 0}

                def flush(back_to, evac_to):
                    while prog["back"] < back_to:
                        sm_back(prog["back"])
                        prog["back"] += 1
                    while prog["evac"] < evac_to:
                        av_evac(prog["evac"])
                        prog["evac"] += 1

                for h in range(H):
                    flush(NT * h, NT * h - 1 if h >= 1 else 0)
                    vector.wait_ge(wkus[h % 2], 16 * (h // 2 + 1))
                    if h >= 1:
                        vector.wait_ge(mkn, h)
                        vector.wait_ge(mv, NG * h)
                    for c in range(NKV):
                        vector.tensor_scalar(knb[:, c, :], wku[:, h % 2, c, 0:128],
                                             kvns[:, c:c + 1], None, ALU.mult)
                        inst = vector.tensor_scalar(vb[:, c, :], wku[:, h % 2, c, 128:256],
                                                    kvns[:, c:c + 1], None, ALU.mult)
                    inst.then_inc(cvk, 1)
                    vector.wait_ge(mkn, h + 1)
                    if h >= 1:
                        vector.wait_ge(mqk, NT * h)
                    for n in range(NS):
                        inst = vector.tensor_copy(knT[:, 512 * n:512 * (n + 1)],
                                                  psA[:, n, :])
                    inst.then_inc(ekn, 1)
                    for g in range(NG):
                        vector.wait_ge(mv, NG * h + g + 1)
                        if h >= 1 and g == 0:
                            vector.wait_ge(mav, NT * h)
                        vector.tensor_copy(v_sb[:, 4 * g:4 * (g + 1), :],
                                           psB[:, :].rearrange("p (a b) -> p a b", a=4)
                                           ).then_inc(evv, 1)
                    for qi in range(NT):
                        idx = NT * h + qi
                        sm_front(h, qi)
                        flush(idx, idx - 1)
                if True:
                    flush(NIDX0, NIDX0)

            @blk.scalar
            def _(scalar):
                def evacs(u):
                    uh, uqi = u // NT, u % NT
                    scalar.wait_ge(mav, max(0, u))
                    for g in range(groups_of(uqi)):
                        cnt = min(8, uqi + 1 - 8 * g)
                        scalar.wait_ge(mtr, gcum[u] + g + 1)
                        scalar.copy(probsT[:, 8 * g:8 * g + cnt, :],
                                    psD[:, 0:cnt, :]).then_inc(etr, 1)
                eprog = [0]

                def flush_evacs(upto):
                    while eprog[0] < upto:
                        evacs(eprog[0])
                        eprog[0] += 1

                for h in range(H):
                    flush_evacs(NT * h)
                    for qi in range(NT):
                        idx = NT * h + qi
                        sl = idx % 2
                        kw = 128 * (qi + 1)
                        flush_evacs(idx - 1)
                        scalar.wait_ge(svr, idx + 1)
                        if idx >= 2:
                            scalar.wait_ge(spr, idx - 1)
                        scalar.activation(probs[:, sl, 0:kw], srow[:, sl, 0:kw],
                                          AF.Exp, bias=stt[:, sl, 0:1], scale=1.0,
                                          accum_out=stt[:, sl, 1:2]).then_inc(sex, 1)
                flush_evacs(NIDX0)

            @blk.tensor
            def _(tensor):
                def qk(h, qi):
                    idx = NT * h + qi
                    ncv = qi // 4 + 1
                    hh = h % 2
                    jj = h // 2
                    if idx >= 1:
                        tensor.wait_ge(sev, idx)
                    for cv in range(ncv):
                        tensor.matmul(psA[:, cv, :],
                                      qT_nope[:, h, 128 * qi:128 * (qi + 1)],
                                      knT[:, 512 * cv:512 * (cv + 1)],
                                      start=True, stop=False)
                        inst = tensor.matmul(
                            psA[:, cv, :],
                            qT_pe[64 * hh:64 * (hh + 1), jj, 128 * qi:128 * (qi + 1)],
                            kpeT[64 * hh:64 * (hh + 1), 512 * cv:512 * (cv + 1)],
                            start=False, stop=True)
                    inst.then_inc(mqk, 1)

                def tail(u):
                    uh, uqi = u // NT, u % NT
                    tensor.wait_ge(spr, u + 1)
                    for g in range(groups_of(uqi)):
                        cnt = min(8, uqi + 1 - 8 * g)
                        if gcum[u] + g >= 1:
                            tensor.wait_ge(etr, gcum[u] + g)
                        for i in range(cnt):
                            inst = tensor.transpose(
                                psD[:, i, :],
                                probs2[:, u % 2, 128 * (8 * g + i):128 * (8 * g + i + 1)],
                                identS[:, :])
                        inst.then_inc(mtr, 1)
                    tensor.wait_ge(etr, gcum[u + 1])
                    if u >= 2:
                        tensor.wait_ge(eav, u - 1)
                    for kj in range(uqi + 1):
                        inst = tensor.matmul(psE[:, u % 2, 0:128], v_sb[:, kj, :],
                                             probsT[:, kj, :],
                                             start=(kj == 0), stop=(kj == uqi))
                    inst.then_inc(mav, 1)

                tprog = {"tail": 0}

                def flush_tails(upto):
                    while tprog["tail"] < upto:
                        tail(tprog["tail"])
                        tprog["tail"] += 1

                for h in range(H):
                    if h >= 1:
                        flush_tails(NT * h)
                    tensor.wait_ge(cvk, h + 1)
                    if h >= 1:
                        tensor.wait_ge(sev, NT * h)
                    for n in range(NS):
                        for k in range(NKV):
                            inst = tensor.matmul(psA[:, n, :], knb[:, k, :],
                                                 kvlT[:, k, 512 * n:512 * (n + 1)],
                                                 start=(k == 0), stop=(k == NKV - 1))
                    inst.then_inc(mkn, 1)
                    for g in range(NG):
                        if h >= 1 or g >= 1:
                            tensor.wait_ge(evv, NG * h + g)
                        for mmi in range(4):
                            tt = 4 * g + mmi
                            for k in range(NKV):
                                inst = tensor.matmul(psB[:, 128 * mmi:128 * (mmi + 1)],
                                                     kvlT[:, k, 128 * tt:128 * (tt + 1)],
                                                     vb[:, k, :],
                                                     start=(k == 0), stop=(k == NKV - 1))
                        inst.then_inc(mv, 1)
                    tensor.wait_ge(ekn, h + 1)
                    for qi in range(NT):
                        idx = NT * h + qi
                        qk(h, qi)
                        flush_tails(idx - 1)
                flush_tails(NIDX0)

            @blk.gpsimd
            def _(g):
                for h in range(H):
                    g.wait_ge(eav, NT * (h + 1))
                    g.dma_start(out=T["attn_oT"][128 * h:128 * (h + 1), :],
                                in_=ahT[:, h % 2, :]).then_inc(hdma[h % 2], 16)
                g.wait_ge(hdma[0], 16 * H // 2)
                g.wait_ge(hdma[1], 16 * H // 2)

        qp_cm.__exit__(None, None, None)
        qn_cm.__exit__(None, None, None)

        with ExitStack() as ctx:
            e = ctx.enter_context
            wo = e(nc.sbuf_tensor(tag + "_wo", [128, NF, D], BF16))
            aoT = e(nc.sbuf_tensor(tag + "_aoT", [128, NF, S], BF16))
            yb = e(nc.sbuf_tensor(tag + "_yb", [128, 2, D], F32))
            ps = e(nc.psum_tensor(tag + "_ps4", [128, 2, 4, 512], F32))
            lw = asem(nc, tag + "_lw5")
            lwg = adsem(nc, tag + "_lwg5")
            mmo = asem(nc, tag + "_mmo")
            evo = asem(nc, tag + "_evo")
            od = [adsem(nc, tag + "_od%d" % i) for i in range(2)]
            blk = e(nc.Block())

            @blk.sync
            def _(sync):
                for i in range(4):
                    sync.dma_start(
                        out=aoT[:, 4 * i:4 * (i + 1), :],
                        in_=T["attn_oT"].ap().rearrange("(c p) s -> p c s", p=128)[:, 4 * i:4 * (i + 1), :],
                    ).then_inc(lw, 16)

            @blk.gpsimd
            def _(g):
                for i in range(4):
                    g.dma_start(
                        out=wo[:, 4 * i:4 * (i + 1), :],
                        in_=T["w_o"].ap().rearrange("(c p) f -> p c f", p=128)[:, 4 * i:4 * (i + 1), :],
                    ).then_inc(lwg, 16)
                for t in range(NT):
                    g.wait_ge(evo, t + 1)
                    g.dma_start(out=T["attn0"][128 * t:128 * (t + 1), :],
                                in_=yb[:, t % 2, :]).then_inc(od[t % 2], 16)
                g.wait_ge(od[0], 16 * NT // 2)
                g.wait_ge(od[1], 16 * NT // 2)

            @blk.tensor
            def _(tensor):
                tensor.wait_ge(lw, 16 * 4)
                tensor.wait_ge(lwg, 16 * 4)
                for t in range(NT):
                    if t >= 2:
                        tensor.wait_ge(evo, t - 1)
                    for n in range(4):
                        for k in range(NF):
                            inst = tensor.matmul(ps[:, t % 2, n, :],
                                                 aoT[:, k, 128 * t:128 * (t + 1)],
                                                 wo[:, k, 512 * n:512 * (n + 1)],
                                                 start=(k == 0), stop=(k == NF - 1))
                    inst.then_inc(mmo, 1)

            @blk.vector
            def _(vector):
                for t in range(NT):
                    vector.wait_ge(mmo, t + 1)
                    if t >= 2:
                        vector.wait_ge(od[t % 2], 16 * (t // 2))
                    for n in range(4):
                        inst = vector.tensor_copy(yb[:, t % 2, 512 * n:512 * (n + 1)],
                                                  ps[:, t % 2, n, :])
                    inst.then_inc(evo, 1)
    return nc



def emit_ffn0(nc, T, tag="f0", parts=(1, 2, 3)):
    _reset_sems(nc)
    E = 8
    I = 1408
    IS = 5632
    with ExitStack() as octx:
        oe = octx.enter_context
        xnT = oe(nc.sbuf_tensor(tag + "_xnT", [128, NF, S], BF16))
        moe = oe(nc.sbuf_tensor(tag + "_moe", [128, NT, D], BF16))
        cmb = oe(nc.sbuf_tensor(tag + "_cmb", [128, NT, E], F32))
        sg = oe(nc.sbuf_tensor(tag + "_sg", [128, NT], F32))
        identS = oe(nc.sbuf_tensor(tag + "_id", [128, 128], BF16))
        ldx = asem(nc, tag + "_ldx")

        with ExitStack() as ctx:
            e = ctx.enter_context
            gw = e(nc.sbuf_tensor(tag + "_gw", [128, NF, E], BF16))
            sgwf = e(nc.sbuf_tensor(tag + "_sgwf", [128, NF], F32))
            sgw = e(nc.sbuf_tensor(tag + "_sgw", [128, NF], BF16))
            sgl = e(nc.sbuf_tensor(tag + "_sgl", [128, NT], F32))
            lg = e(nc.sbuf_tensor(tag + "_lg", [128, E], F32))
            l2 = e(nc.sbuf_tensor(tag + "_l2", [128, E], F32))
            msk = e(nc.sbuf_tensor(tag + "_msk", [128, E], F32))
            em = e(nc.sbuf_tensor(tag + "_em", [128, E], F32))
            stt = e(nc.sbuf_tensor(tag + "_stt", [128, 8], F32))
            psl = e(nc.psum_tensor(tag + "_psl", [128, 2, 512], F32))
            pss = e(nc.psum_tensor(tag + "_pss", [128, 2, 512], F32))
            lw = adsem(nc, tag + "_lw1g")
            lw2 = asem(nc, tag + "_lw2c")
            cvw = asem(nc, tag + "_cvw")
            ml = asem(nc, tag + "_ml")
            evl = asem(nc, tag + "_evl")
            vch = asem(nc, tag + "_vch1")
            sxp = asem(nc, tag + "_sxp")
            scc = asem(nc, tag + "_scc")
            blk = e(nc.Block())
            @blk.sync
            def _(sync):
                for i in range(4):
                    sync.dma_start(
                        out=xnT[:, 4 * i:4 * (i + 1), :],
                        in_=T["xnT"].ap().rearrange("(c p) s -> p c s", p=128)[:, 4 * i:4 * (i + 1), :],
                    ).then_inc(ldx, 16)
                sync.dma_start(out=identS[:, :], in_=T["ident"][:, :]).then_inc(ldx, 16)

            @blk.gpsimd
            def _(g):
                g.dma_start(out=gw[:, :, :],
                            in_=T["gate_wT"].ap().rearrange("(c p) e -> p c e", p=128)
                            ).then_inc(lw, 16)

            @blk.scalar
            def _(scalar):
                pass

            @blk.sync
            def _(sync):
                sync.dma_start(out=sgwf[:, :], in_=T["sgwT"][:, :]).then_inc(lw2, 16)

            @blk.tensor
            def _(tensor):
                tensor.wait_ge(ldx, 16 * 5)
                tensor.wait_ge(lw, 16)
                tensor.wait_ge(cvw, 1)
                for t in range(NT):
                    if t >= 2:
                        tensor.wait_ge(evl, t - 1)
                    for k in range(NF):
                        tensor.matmul(psl[:, t % 2, 0:E],
                                      xnT[:, k, 128 * t:128 * (t + 1)],
                                      gw[:, k, :],
                                      start=(k == 0), stop=(k == NF - 1))
                    for k in range(NF):
                        inst = tensor.matmul(pss[:, t % 2, 0:1],
                                             xnT[:, k, 128 * t:128 * (t + 1)],
                                             sgw[:, k:k + 1],
                                             start=(k == 0), stop=(k == NF - 1))
                    inst.then_inc(ml, 1)

            @blk.vector
            def _(vector):
                nv = 0
                def step(inst):
                    nonlocal nv
                    inst.then_inc(vch, 1)
                    nv += 1
                    vector.wait_ge(vch, nv)
                vector.wait_ge(lw2, 16)
                vector.tensor_copy(sgw[:, :], sgwf[:, :]).then_inc(cvw, 1)
                for t in range(NT):
                    vector.wait_ge(ml, t + 1)
                    step(vector.tensor_copy(lg[:, :], psl[:, t % 2, 0:E]))
                    vector.tensor_copy(sgl[:, t:t + 1], pss[:, t % 2, 0:1]).then_inc(evl, 1)
                    step(vector.tensor_reduce(stt[:, 0:1], lg[:, :], AX.X, ALU.max,
                                              negate=True))
                    step(vector.tensor_scalar(msk[:, :], lg[:, :], stt[:, 0:1],
                                              0.0, ALU.add, ALU.is_ge))
                    step(vector.tensor_scalar(msk[:, :], msk[:, :], -1e30, None,
                                              ALU.mult))
                    step(vector.tensor_tensor(l2[:, :], lg[:, :], msk[:, :], ALU.add))
                    step(vector.tensor_reduce(stt[:, 1:2], l2[:, :], AX.X, ALU.max,
                                              negate=True))
                    step(vector.tensor_tensor(stt[:, 2:3], stt[:, 0:1], stt[:, 1:2],
                                              ALU.subtract))
                    vector.nop().then_inc(sxp, 1)
                    vector.wait_ge(scc, 2 * t + 1)
                    step(vector.tensor_scalar(stt[:, 4:5], stt[:, 3:4], 1.0, None,
                                              ALU.add))
                    step(vector.reciprocal(stt[:, 5:6], stt[:, 4:5]))
                    step(vector.tensor_scalar(msk[:, :], lg[:, :], stt[:, 1:2],
                                              0.0, ALU.add, ALU.is_ge))
                    vector.wait_ge(scc, 2 * t + 2)
                    step(vector.tensor_tensor(em[:, :], em[:, :], msk[:, :], ALU.mult))
                    step(vector.tensor_scalar(cmb[:, t, :], em[:, :], stt[:, 5:6],
                                              None, ALU.mult))

            @blk.scalar
            def _(scalar):
                for t in range(NT):
                    scalar.wait_ge(sxp, t + 1)
                    scalar.activation(stt[:, 3:4], stt[:, 2:3], AF.Exp,
                                      bias=0.0, scale=1.0).then_inc(scc, 1)
                    scalar.activation(em[:, :], lg[:, :], AF.Exp,
                                      bias=stt[:, 0:1], scale=1.0).then_inc(scc, 1)
                scalar.wait_ge(evl, NT)
                scalar.activation(sg[:, :], sgl[:, :], AF.Sigmoid,
                                  bias=0.0, scale=1.0).then_inc(scc, 1)

        groups = []
        for ex in range(E):
            for ns in range(3):
                w = 512 if ns < 2 else 384
                groups.append(("e", ex, ns, w))
        for ns in range(11):
            groups.append(("s", 0, ns, 512))
        NGRP = len(groups)

        with ExitStack() as ctx:
            e = ctx.enter_context
            wg = e(nc.sbuf_tensor(tag + "_wg", [128, 2, NF, 512], BF16))
            wu = e(nc.sbuf_tensor(tag + "_wu", [128, NF, 512], BF16))
            wd = e(nc.sbuf_tensor(tag + "_wd", [128, 4, D], BF16))
            hg = e(nc.sbuf_tensor(tag + "_hg", [128, 2, 512], BF16))
            ub = e(nc.sbuf_tensor(tag + "_ub", [128, 2, 512], BF16))
            hb = e(nc.sbuf_tensor(tag + "_hb", [128, 2, 512], BF16))
            hT = e(nc.sbuf_tensor(tag + "_hT", [128, 2, 4, 128], BF16))
            pg = e(nc.psum_tensor(tag + "_pg", [128, 2, 512], F32))
            pu = e(nc.psum_tensor(tag + "_pu", [128, 2, 512], F32))
            psT = e(nc.psum_tensor(tag + "_psT", [128, 2, 4, 128], BF16))
            pd = e(nc.psum_tensor(tag + "_pd", [128, 2, 512], F32))
            wsg = [adsem(nc, tag + "_wsg%d" % i) for i in range(2)]
            wsu = adsem(nc, tag + "_wsu")
            wsd = adsem(nc, tag + "_wsd")
            mmg = asem(nc, tag + "_mmg")
            sact = asem(nc, tag + "_sact")
            evgu = asem(nc, tag + "_evgu")
            vh = asem(nc, tag + "_vh")
            mtr = asem(nc, tag + "_mtr")
            evh = asem(nc, tag + "_evh")
            mmd = asem(nc, tag + "_mmd")
            emoe = asem(nc, tag + "_emoe")
            vmz = asem(nc, tag + "_vmz")
            blk = e(nc.Block())

            @blk.gpsimd
            def _(g):
                for gi, (kind, ex, ns, w) in enumerate(groups):
                    if gi >= 2:
                        g.wait_ge(mmg, NT * gi)
                    if kind == "e":
                        g.dma_start(
                            out=wg[:, gi % 2, :, 0:w],
                            in_=T["w_gate"].ap()[ex].rearrange("(c p) i -> p c i", p=128)[:, :, 512 * ns:512 * ns + w],
                        ).then_inc(wsg[gi % 2], 16)
                    else:
                        g.dma_start(
                            out=wg[:, gi % 2, :, 0:w],
                            in_=T["sw_gate"].ap().rearrange("(c p) i -> p c i", p=128)[:, :, 512 * ns:512 * ns + w],
                        ).then_inc(wsg[gi % 2], 16)
                    if gi >= 1:
                        g.wait_ge(mmg, NT * gi)
                    if kind == "e":
                        g.dma_start(
                            out=wu[:, :, 0:w],
                            in_=T["w_up"].ap()[ex].rearrange("(c p) i -> p c i", p=128)[:, :, 512 * ns:512 * ns + w],
                        ).then_inc(wsu, 16)
                    else:
                        g.dma_start(
                            out=wu[:, :, 0:w],
                            in_=T["sw_up"].ap().rearrange("(c p) i -> p c i", p=128)[:, :, 512 * ns:512 * ns + w],
                        ).then_inc(wsu, 16)
                    if gi >= 1:
                        g.wait_ge(mmd, 4 * NT * gi)
                    if kind == "e":
                        g.dma_start(
                            out=wd[:, 0:(w + 127) // 128, :],
                            in_=T["w_down"].ap()[ex].rearrange("(c p) o -> p c o", p=128)[:, 4 * ns:4 * ns + (w + 127) // 128, :],
                        ).then_inc(wsd, 16)
                    else:
                        g.dma_start(
                            out=wd[:, 0:4, :],
                            in_=T["sw_down"].ap().rearrange("(c p) o -> p c o", p=128)[:, 4 * ns:4 * ns + 4, :],
                        ).then_inc(wsd, 16)

            @blk.vector
            def _(vector):
                for t in range(NT):
                    vector.memset(moe[:, t, :], 0.0)
                vector.nop().then_inc(vmz, 1)
                for gi, (kind, ex, ns, w) in enumerate(groups):
                    nk = (w + 127) // 128
                    for t in range(NT):
                        idx = NT * gi + t
                        sl = t % 2
                        vector.wait_ge(mmg, idx + 1)
                        if kind == "e":
                            csl = cmb[:, t, ex:ex + 1]
                        else:
                            csl = sg[:, t:t + 1]
                        vector.tensor_scalar(ub[:, sl, 0:w], pu[:, sl, 0:w], csl,
                                             None, ALU.mult).then_inc(evgu, 1)
                        vector.wait_ge(evgu, idx + 1)
                        vector.wait_ge(sact, idx + 1)
                        vector.tensor_tensor(hb[:, sl, 0:w], hg[:, sl, 0:w],
                                             ub[:, sl, 0:w], ALU.mult).then_inc(vh, 1)
                        vector.wait_ge(mtr, idx + 1)
                        vector.tensor_copy(hT[:, sl, 0:nk, :],
                                           psT[:, sl, 0:nk, :]).then_inc(evh, 1)
                        for n in range(4):
                            vector.wait_ge(mmd, 4 * idx + n + 1)
                            vector.tensor_tensor(moe[:, t, 512 * n:512 * (n + 1)],
                                                 moe[:, t, 512 * n:512 * (n + 1)],
                                                 pd[:, n % 2, :],
                                                 ALU.add).then_inc(emoe, 1)

            @blk.scalar
            def _(scalar):
                for gi, (kind, ex, ns, w) in enumerate(groups):
                    for t in range(NT):
                        idx = NT * gi + t
                        sl = t % 2
                        scalar.wait_ge(mmg, idx + 1)
                        scalar.activation(hg[:, sl, 0:w], pg[:, sl, 0:w], AF.Silu,
                                          bias=0.0, scale=1.0).then_inc(sact, 1)

            @blk.tensor
            def _(tensor):
                tensor.wait_ge(vmz, 1)

                def epilogue(gi, kind, w, t):
                    idx = NT * gi + t
                    sl = t % 2
                    nk = (w + 127) // 128
                    tensor.wait_ge(vh, idx + 1)
                    if idx >= 2:
                        tensor.wait_ge(evh, idx - 1)
                    for c2_ in range(nk):
                        inst = tensor.transpose(psT[:, sl, c2_, :],
                                                hb[:, sl, 128 * c2_:128 * (c2_ + 1)],
                                                identS[:, :])
                    inst.then_inc(mtr, 1)
                    tensor.wait_ge(evh, idx + 1)
                    for n in range(4):
                        if 4 * idx + n >= 2:
                            tensor.wait_ge(emoe, 4 * idx + n - 1)
                        for c2_ in range(nk):
                            inst = tensor.matmul(pd[:, n % 2, :],
                                                 hT[:, sl, c2_, :],
                                                 wd[:, c2_, 512 * n:512 * (n + 1)],
                                                 start=(c2_ == 0), stop=(c2_ == nk - 1))
                        inst.then_inc(mmd, 1)

                for gi, (kind, ex, ns, w) in enumerate(groups):
                    tensor.wait_ge(wsg[gi % 2], 16 * (gi // 2 + 1))
                    tensor.wait_ge(wsu, 16 * (gi + 1))
                    for t in range(NT):
                        idx = NT * gi + t
                        sl = t % 2
                        if idx >= 2:
                            tensor.wait_ge(evgu, idx - 1)
                            tensor.wait_ge(sact, idx - 1)
                        if t == 0:
                            tensor.wait_ge(wsd, 16 * (gi + 1))
                        for k in range(NF):
                            tensor.matmul(pg[:, sl, 0:w],
                                          xnT[:, k, 128 * t:128 * (t + 1)],
                                          wg[:, gi % 2, k, 0:w],
                                          start=(k == 0), stop=(k == NF - 1))
                        for k in range(NF):
                            inst = tensor.matmul(pu[:, sl, 0:w],
                                                 xnT[:, k, 128 * t:128 * (t + 1)],
                                                 wu[:, k, 0:w],
                                                 start=(k == 0), stop=(k == NF - 1))
                        inst.then_inc(mmg, 1)
                        if t >= 1:
                            epilogue(gi, kind, w, t - 1)
                    epilogue(gi, kind, w, NT - 1)

        with ExitStack() as ctx:
            e = ctx.enter_context
            xf = e(nc.sbuf_tensor(tag + "_xf", [128, 2, D], F32))
            af = e(nc.sbuf_tensor(tag + "_af", [128, 2, D], F32))
            yv = e(nc.sbuf_tensor(tag + "_yv", [128, 2, D], F32))
            lx = [asem(nc, tag + "_lx%d" % i) for i in range(2)]
            la = [asem(nc, tag + "_la%d" % i) for i in range(2)]
            vy = asem(nc, tag + "_vy")
            vc4 = asem(nc, tag + "_vc4")
            oy = [adsem(nc, tag + "_oy%d" % i) for i in range(2)]
            blk = e(nc.Block())

            @blk.sync
            def _(sync):
                for t in range(NT):
                    if t >= 2:
                        sync.wait_ge(vy, t - 1)
                    sync.dma_start(out=xf[:, t % 2, :],
                                   in_=T["x"][128 * t:128 * (t + 1), :]).then_inc(lx[t % 2], 16)
                    sync.dma_start(out=af[:, t % 2, :],
                                   in_=T["attn0"][128 * t:128 * (t + 1), :]).then_inc(la[t % 2], 16)

            @blk.vector
            def _(vector):
                nv = 0
                for t in range(NT):
                    vector.wait_ge(lx[t % 2], 16 * (t // 2 + 1))
                    vector.wait_ge(la[t % 2], 16 * (t // 2 + 1))
                    if t >= 2:
                        vector.wait_ge(oy[t % 2], 16 * (t // 2))
                    vector.tensor_tensor(yv[:, t % 2, :], xf[:, t % 2, :],
                                         af[:, t % 2, :], ALU.add).then_inc(vc4, 1)
                    nv += 1
                    vector.wait_ge(vc4, nv)
                    vector.tensor_tensor(yv[:, t % 2, :], yv[:, t % 2, :],
                                         moe[:, t, :], ALU.add).then_inc(vy, 1)

            @blk.gpsimd
            def _(g):
                for t in range(NT):
                    g.wait_ge(vy, t + 1)
                    g.dma_start(out=T["y0"][128 * t:128 * (t + 1), :],
                                in_=yv[:, t % 2, :]).then_inc(oy[t % 2], 16)
                g.wait_ge(oy[0], 16 * NT // 2)
                g.wait_ge(oy[1], 16 * NT // 2)
    return nc



def emit_attn1(nc, T, tag="a1"):
    _reset_sems(nc)
    NQ = 4
    NH = 8
    DH = 64
    NS = S // 512
    with ExitStack() as octx:
        oe = octx.enter_context
        kT = oe(nc.sbuf_tensor(tag + "_kT", [128, 4, S], BF16))
        v_sb = oe(nc.sbuf_tensor(tag + "_v", [128, NT, 512], BF16))
        identS = oe(nc.sbuf_tensor(tag + "_id", [128, 128], BF16))
        wcol = oe(nc.sbuf_tensor(tag + "_wcol", [128, 4], F32))
        ldc = asem(nc, tag + "_ldc")
        qT_cm = nc.sbuf_tensor(tag + "_qT", [128, NF, S], BF16, side="right")
        qT = qT_cm.__enter__()

        with ExitStack() as ctx:
            e = ctx.enter_context
            xnT = e(nc.sbuf_tensor(tag + "_xnT", [128, NF, S], BF16))
            wkb = e(nc.sbuf_tensor(tag + "_wkb", [128, NF, 512], BF16))
            wvb = e(nc.sbuf_tensor(tag + "_wvb", [128, NF, 512], BF16))
            wqb = e(nc.sbuf_tensor(tag + "_wqb", [128, 2, NF, 128], BF16))
            mwf = e(nc.sbuf_tensor(tag + "_mwf", [1, 1], F32))
            mws = e(nc.sbuf_tensor(tag + "_mws", [1, 1], F32))
            onesb = e(nc.sbuf_tensor(tag + "_ones", [1, 128], F32))
            ps = e(nc.psum_tensor(tag + "_ps1", [128, 4, 512], F32))
            psv = e(nc.psum_tensor(tag + "_psv", [128, 2, 512], F32))
            lw = asem(nc, tag + "_lw1")
            lws = asem(nc, tag + "_lws1")
            lwg = adsem(nc, tag + "_lwg1")
            wqs = [adsem(nc, tag + "_wqs%d" % i) for i in range(2)]
            mmq = asem(nc, tag + "_mmq1")
            evq = asem(nc, tag + "_evq1")
            mmv = asem(nc, tag + "_mmv")
            evv = asem(nc, tag + "_evv1")
            ssg = asem(nc, tag + "_ssg")
            mwm = asem(nc, tag + "_mwm")
            evw = asem(nc, tag + "_evw")
            blk = e(nc.Block())

            @blk.sync
            def _(sync):
                sync.dma_start(out=identS[:, :], in_=T["ident"][:, :]).then_inc(ldc, 16)
                sync.dma_start(out=mwf[:, :], in_=T["mw"][:, :]).then_inc(lws, 16)
                sync.dma_start(out=onesb[:, :], in_=T["ones1f"][:, :]).then_inc(lws, 16)
                for i in range(4):
                    sync.dma_start(
                        out=xnT[:, 4 * i:4 * (i + 1), :],
                        in_=T["xnT"].ap().rearrange("(c p) s -> p c s", p=128)[:, 4 * i:4 * (i + 1), :],
                    ).then_inc(lw, 16)

            @blk.gpsimd
            def _(g):
                g.dma_start(out=wkb[:, :, :],
                            in_=T["WkT"].ap().rearrange("(c p) f -> p c f", p=128)
                            ).then_inc(lwg, 16)
                g.dma_start(out=wvb[:, :, :],
                            in_=T["WvT"].ap().rearrange("(c p) f -> p c f", p=128)
                            ).then_inc(lwg, 16)
                for m in range(NF):
                    if m >= 2:
                        g.wait_ge(mmq, m - 1)
                    g.dma_start(
                        out=wqb[:, m % 2, :, :],
                        in_=T["WqT"].ap().rearrange("(c p) f -> p c f", p=128)[:, :, 128 * m:128 * (m + 1)],
                    ).then_inc(wqs[m % 2], 16)

            @blk.scalar
            def _(scalar):
                scalar.wait_ge(lws, 32)
                scalar.activation(mws[:, :], mwf[:, :], AF.Sigmoid,
                                  bias=0.0, scale=1.0).then_inc(ssg, 1)

            @blk.tensor
            def _(tensor):
                tensor.wait_ge(lw, 16 * 4)
                tensor.wait_ge(lws, 32)
                tensor.wait_ge(lwg, 32)
                tensor.wait_ge(ssg, 1)
                tensor.matmul(psv[:, 0, 0:1], onesb[:, :], mws[:, :],
                              start=True, stop=True).then_inc(mwm, 1)
                tensor.wait_ge(evw, 2)
                for m in range(NF):
                    tensor.wait_ge(wqs[m % 2], 16 * (m // 2 + 1))
                    if m >= 1:
                        tensor.wait_ge(evq, m)
                    for n in range(NS):
                        for k in range(NF):
                            inst = tensor.matmul(ps[:, n, :],
                                                 wqb[:, m % 2, k, :],
                                                 xnT[:, k, 512 * n:512 * (n + 1)],
                                                 start=(k == 0), stop=(k == NF - 1))
                    inst.then_inc(mmq, 1)
                for m in range(4):
                    tensor.wait_ge(evq, NF + m)
                    for n in range(NS):
                        for k in range(NF):
                            inst = tensor.matmul(ps[:, n, :],
                                                 wkb[:, k, 128 * m:128 * (m + 1)],
                                                 xnT[:, k, 512 * n:512 * (n + 1)],
                                                 start=(k == 0), stop=(k == NF - 1))
                    inst.then_inc(mmq, 1)
                for t in range(NT):
                    if t >= 2:
                        tensor.wait_ge(evv, t - 1)
                    for k in range(NF):
                        inst = tensor.matmul(psv[:, t % 2, :],
                                             xnT[:, k, 128 * t:128 * (t + 1)],
                                             wvb[:, k, :],
                                             start=(k == 0), stop=(k == NF - 1))
                    inst.then_inc(mmv, 1)

            @blk.vector
            def _(vector):
                vector.wait_ge(mwm, 1)
                vector.tensor_copy(wcol[:, 0:1], psv[:, 0, 0:1]).then_inc(evw, 1)
                vector.wait_ge(evw, 1)
                vector.tensor_scalar(wcol[:, 1:2], wcol[:, 0:1], -1.0, 1.0,
                                     ALU.mult, ALU.add).then_inc(evw, 1)
                for m in range(NF):
                    vector.wait_ge(mmq, m + 1)
                    for n in range(NS):
                        inst = vector.tensor_copy(qT[:, m, 512 * n:512 * (n + 1)],
                                                  ps[:, n, :])
                    inst.then_inc(evq, 1)
                for m in range(4):
                    vector.wait_ge(mmq, NF + m + 1)
                    for n in range(NS):
                        inst = vector.tensor_copy(kT[:, m, 512 * n:512 * (n + 1)],
                                                  ps[:, n, :])
                    inst.then_inc(evq, 1)
                for t in range(NT):
                    vector.wait_ge(mmv, t + 1)
                    vector.tensor_copy(v_sb[:, t, :], psv[:, t % 2, :]).then_inc(evv, 1)

        with ExitStack() as ctx:
            e = ctx.enter_context
            srow = e(nc.sbuf_tensor(tag + "_srow", [128, 2, S], F32))
            probs = e(nc.sbuf_tensor(tag + "_probs", [128, 2, S], BF16))
            probsT = e(nc.sbuf_tensor(tag + "_probsT", [128, NT, 128], BF16))
            adot = e(nc.sbuf_tensor(tag + "_adot", [128, NT, D], BF16))
            stt = e(nc.sbuf_tensor(tag + "_stt", [128, 3, 4], F32))
            psA = e(nc.psum_tensor(tag + "_psA", [128, 4, 512], F32))
            psD = e(nc.psum_tensor(tag + "_psD", [128, 8, 128], BF16))
            psE = e(nc.psum_tensor(tag + "_psE", [128, 2, 512], F32))
            mqk = asem(nc, tag + "_mqk")
            sev = asem(nc, tag + "_sev")
            svr = asem(nc, tag + "_svr")
            sex = asem(nc, tag + "_sex")
            mtr = asem(nc, tag + "_mtr")
            etr = asem(nc, tag + "_etr")
            mav = asem(nc, tag + "_mav")
            eav = asem(nc, tag + "_eav")
            oad = adsem(nc, tag + "_oad")
            vch = asem(nc, tag + "_vch2")
            blk = e(nc.Block())

            HEADS = [(nq, hh) for nq in range(NQ) for hh in range(NH)]
            NH_ALL = len(HEADS)
            NTG = (NT + 7) // 8
            NIDX = NT * NH_ALL

            def head_of(idx):
                nq, hh = HEADS[idx // NT]
                return nq, hh, idx % NT

            @blk.tensor
            def _(tensor):
                tensor.wait_ge(ldc, 16)

                def qk(idx):
                    nq, hh, qi = head_of(idx)
                    qc = 4 * nq + hh // 2
                    kc = hh // 2
                    base = 64 * (hh % 2)
                    if idx >= 1:
                        tensor.wait_ge(sev, idx)
                    for n in range(NS):
                        inst = tensor.matmul(psA[:, n, :],
                                             qT[base:base + 64, qc, 128 * qi:128 * (qi + 1)],
                                             kT[base:base + 64, kc, 512 * n:512 * (n + 1)],
                                             start=True, stop=True)
                    inst.then_inc(mqk, 1)

                def tail(idx):
                    nq, hh, qi = head_of(idx)
                    for g in range(NTG):
                        cnt = min(8, NT - 8 * g)
                        tensor.wait_ge(sex, idx + 1)
                        if NTG * idx + g >= 1:
                            tensor.wait_ge(etr, NTG * idx + g)
                        for i in range(cnt):
                            kj = 8 * g + i
                            inst = tensor.transpose(psD[:, i, :],
                                                    probs[:, idx % 2, 128 * kj:128 * (kj + 1)],
                                                    identS[:, :])
                        inst.then_inc(mtr, 1)
                    tensor.wait_ge(etr, NTG * (idx + 1))
                    if idx >= 2:
                        tensor.wait_ge(eav, idx - 1)
                    for kj in range(NT):
                        inst = tensor.matmul(psE[:, idx % 2, 0:64],
                                             probsT[:, kj, :],
                                             v_sb[:, kj, 64 * hh:64 * (hh + 1)],
                                             start=(kj == 0), stop=(kj == NT - 1))
                    inst.then_inc(mav, 1)

                for idx in range(NIDX):
                    qk(idx)
                    if idx >= 2:
                        tail(idx - 2)
                tail(NIDX - 2)
                tail(NIDX - 1)

            def _emit_adot_evac(vector, step, idx):
                nq, hh, qi = head_of(idx)
                s3 = idx % 3
                vector.wait_ge(mav, idx + 1)
                step(vector.reciprocal(stt[:, s3, 2:3], stt[:, s3, 1:2]))
                vector.tensor_scalar(
                    adot[:, qi, 512 * nq + 64 * hh:512 * nq + 64 * (hh + 1)],
                    psE[:, idx % 2, 0:64], stt[:, s3, 2:3], wcol[:, 1:2],
                    ALU.mult, ALU.mult).then_inc(eav, 1)

            @blk.vector
            def _(vector):
                nv = [0]
                def step(inst):
                    nv[0] += 1
                    inst.then_inc(vch, 1)
                    vector.wait_ge(vch, nv[0])
                for idx in range(NIDX):
                    nq, hh, qi = head_of(idx)
                    sl = idx % 2
                    vector.wait_ge(mqk, idx + 1)
                    if idx >= 2:
                        vector.wait_ge(sex, idx - 1)
                    for n in range(NS):
                        inst = vector.tensor_scalar(srow[:, sl, 512 * n:512 * (n + 1)],
                                                    psA[:, n, :], 0.125, None,
                                                    ALU.mult)
                    inst.then_inc(sev, 1)
                    vector.wait_ge(sev, idx + 1)
                    vector.tensor_reduce(stt[:, idx % 3, 0:1], srow[:, sl, :], AX.X,
                                         ALU.max, negate=True).then_inc(svr, 1)
                    if idx >= 2:
                        _emit_adot_evac(vector, step, idx - 2)
                _emit_adot_evac(vector, step, NIDX - 2)
                _emit_adot_evac(vector, step, NIDX - 1)

            @blk.scalar
            def _(scalar):
                def evacs(u):
                    scalar.wait_ge(mav, max(0, u))
                    for g in range(NTG):
                        cnt = min(8, NT - 8 * g)
                        scalar.wait_ge(mtr, NTG * u + g + 1)
                        scalar.copy(probsT[:, 8 * g:8 * g + cnt, :],
                                    psD[:, 0:cnt, :]).then_inc(etr, 1)
                for idx in range(NIDX):
                    sl = idx % 2
                    if idx >= 2:
                        evacs(idx - 2)
                    scalar.wait_ge(svr, idx + 1)
                    if idx >= 2:
                        scalar.wait_ge(mtr, NTG * (idx - 1))
                    scalar.activation(probs[:, sl, :], srow[:, sl, :], AF.Exp,
                                      bias=stt[:, idx % 3, 0:1], scale=1.0,
                                      accum_out=stt[:, idx % 3, 1:2]).then_inc(sex, 1)
                evacs(NIDX - 2)
                evacs(NIDX - 1)

            @blk.gpsimd
            def _(g):
                g.wait_ge(eav, NIDX)
                for t in range(NT):
                    g.dma_start(out=T["adot"][128 * t:128 * (t + 1), :],
                                in_=adot[:, t, :]).then_inc(oad, 16)
                g.wait_ge(oad, 16 * NT)

        with ExitStack() as ctx:
            e = ctx.enter_context
            memb = e(nc.sbuf_tensor(tag + "_memb", [128, 4, 512], BF16))
            mnb = e(nc.sbuf_tensor(tag + "_mnb", [128, 4], BF16))
            mnf = e(nc.sbuf_tensor(tag + "_mnf", [128, 4], F32))
            sig = e(nc.sbuf_tensor(tag + "_sig", [128, 4, S], BF16))
            t1b = e(nc.sbuf_tensor(tag + "_t1b", [128, S], F32))
            t2b = e(nc.sbuf_tensor(tag + "_t2b", [128, S], F32))
            t3b = e(nc.sbuf_tensor(tag + "_t3b", [128, S], F32))
            dcol = e(nc.sbuf_tensor(tag + "_dcol", [128, 2], F32))
            am = e(nc.sbuf_tensor(tag + "_am", [128, 2, 512], F32))
            psM = e(nc.psum_tensor(tag + "_psM", [128, 2, 512], F32))
            psN = e(nc.psum_tensor(tag + "_psN", [128, 2, 512], F32))
            lwm = adsem(nc, tag + "_lwm")
            lwn = asem(nc, tag + "_lwn")
            cvn = asem(nc, tag + "_cvnm")
            selu = asem(nc, tag + "_selu")
            velu = asem(nc, tag + "_velu")
            mmm = asem(nc, tag + "_mmm")
            evm = asem(nc, tag + "_evm")
            oam = [asem(nc, tag + "_oam%d" % i) for i in range(2)]
            vch3 = asem(nc, tag + "_vch3")
            blk = e(nc.Block())

            @blk.gpsimd
            def _(g):
                g.dma_start(out=memb[:, :, :],
                            in_=T["memory"].ap().rearrange("(c p) f -> p c f", p=128)
                            ).then_inc(lwm, 16)

            @blk.scalar
            def _(scalar):
                for qc in range(NF):
                    if qc >= 1:
                        scalar.wait_ge(velu, qc)
                    scalar.activation(t1b[:, :], qT[:, qc, :], AF.Relu,
                                      bias=0.0, scale=1.0).then_inc(selu, 1)
                    scalar.activation(t2b[:, :], qT[:, qc, :], AF.Relu,
                                      bias=0.0, scale=np.float32(-1.0)).then_inc(selu, 1)
                    scalar.wait_ge(selu, 3 * qc + 2)
                    scalar.activation(t3b[:, :], t2b[:, :], AF.Exp,
                                      bias=0.0, scale=np.float32(-1.0)).then_inc(selu, 1)

            @blk.vector
            def _(vector):
                nv = 0
                def step(inst):
                    nonlocal nv
                    inst.then_inc(vch3, 1)
                    nv += 1
                    vector.wait_ge(vch3, nv)
                vector.wait_ge(lwn, 16)
                vector.tensor_copy(mnb[:, :], mnf[:, :]).then_inc(cvn, 1)
                for qc in range(NF):
                    vector.wait_ge(selu, 3 * qc + 3)
                    step(vector.tensor_tensor(t1b[:, :], t1b[:, :], t3b[:, :],
                                              ALU.add))
                    vector.tensor_scalar(sig[:, qc % 4, :], t1b[:, :], -1.0, None,
                                         ALU.add).then_inc(velu, 1)
                    if qc % 4 == 3:
                        nq = qc // 4
                        for t in range(NT):
                            idx = NT * nq + t
                            vector.wait_ge(mmm, idx + 1)
                            step(vector.tensor_copy(dcol[:, 0:1], psN[:, t % 2, 0:1]))
                            step(vector.reciprocal(dcol[:, 1:2], dcol[:, 0:1]))
                            if idx >= 2:
                                vector.wait_ge(oam[t % 2], 16 * (idx // 2))
                            vector.tensor_scalar(am[:, t % 2, :], psM[:, t % 2, :],
                                                 dcol[:, 1:2], wcol[:, 0:1],
                                                 ALU.mult, ALU.mult).then_inc(evm, 1)

            @blk.tensor
            def _(tensor):
                tensor.wait_ge(lwm, 16)
                tensor.wait_ge(cvn, 1)
                for nq in range(NQ):
                    tensor.wait_ge(velu, 4 * (nq + 1))
                    for t in range(NT):
                        idx = NT * nq + t
                        if idx >= 2:
                            tensor.wait_ge(evm, idx - 1)
                        for c in range(4):
                            tensor.matmul(psM[:, t % 2, :],
                                          sig[:, c, 128 * t:128 * (t + 1)],
                                          memb[:, c, :],
                                          start=(c == 0), stop=(c == 3))
                        for c in range(4):
                            inst = tensor.matmul(psN[:, t % 2, 0:1],
                                                 sig[:, c, 128 * t:128 * (t + 1)],
                                                 mnb[:, c:c + 1],
                                                 start=(c == 0), stop=(c == 3))
                        inst.then_inc(mmm, 1)

            @blk.sync
            def _(sync):
                sync.dma_start(out=mnf[:, :], in_=T["mnT"][:, :]).then_inc(lwn, 16)
                for nq in range(NQ):
                    for t in range(NT):
                        idx = NT * nq + t
                        sync.wait_ge(evm, idx + 1)
                        sync.dma_start(
                            out=T["amem"][128 * t:128 * (t + 1), 512 * nq:512 * (nq + 1)],
                            in_=am[:, t % 2, :]).then_inc(oam[t % 2], 16)
                sync.wait_ge(oam[0], 16 * NQ * NT // 2)
                sync.wait_ge(oam[1], 16 * NQ * NT // 2)

        qT_cm.__exit__(None, None, None)
    return nc



def emit_ffn1(nc, T, tag="f1"):
    _reset_sems(nc)
    NR = 11
    NSS = S // 512
    UN = 4 * NSS
    with ExitStack() as octx:
        oe = octx.enter_context
        mctx = ExitStack()
        me = mctx.enter_context
        xnT = me(nc.sbuf_tensor(tag + "_xnT", [128, NF, S], BF16))
        identS = me(nc.sbuf_tensor(tag + "_id", [128, 128], BF16))
        gusT = me(nc.sbuf_tensor(tag + "_gusT", [128, 88, 16], F16))
        dsT = me(nc.sbuf_tensor(tag + "_dsT", [128, 16, 44], F16))
        gTg = me(nc.sbuf_tensor(tag + "_gTg", [128, NF, 512], BF16))
        gTu = me(nc.sbuf_tensor(tag + "_gTu", [128, NF, 512], BF16))
        hT = me(nc.sbuf_tensor(tag + "_hT", [128, 4, S], BF16))
        wdT = me(nc.sbuf_tensor(tag + "_wdT", [128, 4, D], BF16))
        pk = me(nc.sbuf_tensor(tag + "_pk", [128, 2, 256], I32))
        A_ = me(nc.sbuf_tensor(tag + "_A", [128, 4, 512], U16))
        B_ = me(nc.sbuf_tensor(tag + "_B", [128, 4, 512], U16))
        C_ = me(nc.sbuf_tensor(tag + "_C", [128, 4, 512], U16))
        val2g = me(nc.sbuf_tensor(tag + "_val2g", [128, 8, 2048], BF16))
        val2d = me(nc.sbuf_tensor(tag + "_val2d", [128, 16, 512], BF16))
        scf = me(nc.sbuf_tensor(tag + "_scf", [128, 16], F32))
        hg = me(nc.sbuf_tensor(tag + "_hg", [128, 512], BF16))
        yev = me(nc.sbuf_tensor(tag + "_yev", [128, 2, D], BF16))
        psT = me(nc.psum_tensor(tag + "_psT", [128, 2, 8, 128], BF16))
        pgu = me(nc.psum_tensor(tag + "_pgu", [128, 2, 2, 512], F32))
        pyp = me(nc.psum_tensor(tag + "_pyp", [128, 2, 512], F32))
        ldx = asem(nc, tag + "_ldx")
        lsc = asem(nc, tag + "_lsc")
        pkS = [asem(nc, tag + "_pk%d" % i) for i in range(2)]
        vch = asem(nc, tag + "_vch")
        vdq = asem(nc, tag + "_vdq")
        mtq = asem(nc, tag + "_mtq")
        etq = asem(nc, tag + "_etq")
        mgu = asem(nc, tag + "_mgu")
        sact = asem(nc, tag + "_sact")
        vh2 = asem(nc, tag + "_vh2")
        myp = asem(nc, tag + "_myp")
        eyp = asem(nc, tag + "_eyp")
        oyd = [adsem(nc, tag + "_oyd%d" % i) for i in range(2)]

        def ocs_of_round(r):
            out = []
            for j in range(4):
                out.append(("g", 4 * r + j, j))
            for j in range(4):
                out.append(("u", 44 + 4 * r + j, j))
            for j in range(16):
                out.append(("d", j, j))
            return out

        SCHED = []
        nli = 0
        wav = 0
        MTQ_AFTER = []
        for r in range(NR):
            for (kind, oc, j) in ocs_of_round(r):
                nwv = 2 if kind in "gu" else 1
                SCHED.append((r, kind, oc, j, nli, wav))
                wav += nwv
                MTQ_AFTER.append(wav)
                nli += 1
        WAVES_PER_ROUND = 32

        with nc.Block() as blk:

            @blk.sync
            def _(sync):
                for i in range(4):
                    sync.dma_start(
                        out=xnT[:, 4 * i:4 * (i + 1), :],
                        in_=T["xnT"].ap().rearrange("(c p) s -> p c s", p=128)[:, 4 * i:4 * (i + 1), :],
                    ).then_inc(ldx, 16)
                sync.dma_start(out=identS[:, :], in_=T["ident"][:, :]).then_inc(ldx, 16)
                sync.dma_start(out=gusT[:, :, :],
                               in_=T["gusT"].ap().rearrange("(c p) s -> p c s", p=128)
                               ).then_inc(lsc, 16)
                sync.dma_start(out=dsT[:, :, :],
                               in_=T["dsT"].ap().rearrange("(c p) s -> p c s", p=128)
                               ).then_inc(lsc, 16)
                for (r, kind, oc, j, li, wb) in SCHED:
                    sl = li % 2
                    if li >= 2:
                        sync.wait_ge(vdq, li - 1)
                    if kind in "gu":
                        sync.dma_start(
                            out=pk[:, sl, :],
                            in_=T["gup"].ap().rearrange("(c p) w -> p c w", p=128)[:, oc, :],
                        ).then_inc(pkS[sl], 16)
                    else:
                        sync.dma_start(
                            out=pk[:, sl, 0:64],
                            in_=T["dwp"].ap().rearrange("(c p) w -> p c w", p=128)[:, oc, 64 * r:64 * (r + 1)],
                        ).then_inc(pkS[sl], 16)

            @blk.vector
            def _(vector):
                nv = [0]
                def step(inst):
                    nv[0] += 1
                    inst.then_inc(vch, 1)
                    vector.wait_ge(vch, nv[0])
                vector.wait_ge(lsc, 32)

                def deq_oc(r, kind, oc, j, li):
                    sl = li % 2
                    H = 512 if kind in "gu" else 128
                    vector.wait_ge(pkS[sl], 16 * (li // 2 + 1))
                    u16v = pk[:, sl, 0:H // 2].bitcast(U16)
                    for k in range(4):
                        if k < 3:
                            step(vector.tensor_scalar(A_[:, k, 0:H], u16v, 4 * k,
                                                      15, ALU.logical_shift_right,
                                                      ALU.bitwise_and))
                        else:
                            step(vector.tensor_scalar(A_[:, k, 0:H], u16v, 12,
                                                      None,
                                                      ALU.logical_shift_right))
                    step(vector.tensor_scalar(B_[:, :, 0:H], A_[:, :, 0:H], 12,
                                              0x8000, ALU.logical_shift_left,
                                              ALU.bitwise_and))
                    step(vector.tensor_scalar(A_[:, :, 0:H], A_[:, :, 0:H], 7, 6,
                                              ALU.bitwise_and,
                                              ALU.logical_shift_left))
                    step(vector.tensor_tensor(B_[:, :, 0:H], B_[:, :, 0:H],
                                              A_[:, :, 0:H], ALU.add))
                    step(vector.tensor_scalar(B_[:, :, 0:H], B_[:, :, 0:H],
                                              0x3F00, None, ALU.add))
                    Bv = B_[:, :, 0:H].bitcast(BF16)
                    Av = A_[:, :, 0:H].bitcast(BF16)
                    Cv = C_[:, :, 0:H].bitcast(BF16)
                    step(vector.tensor_scalar(Cv, Bv, -0.5, 0.5, ALU.max, ALU.min))
                    step(vector.tensor_scalar(Cv, Cv, -2.0, None, ALU.mult))
                    step(vector.tensor_scalar(Av, Bv, -1.0, 1.0, ALU.max, ALU.min))
                    step(vector.tensor_tensor(Av, Av, Cv, ALU.add))
                    dstv = (val2g[:, li % 24, :] if kind in "gu"
                            else val2d[:, j, :])
                    step(vector.tensor_tensor(
                        dstv.rearrange("p (h k) -> p h k", k=4),
                        Bv.transpose([0, 2, 1]),
                        Av.transpose([0, 2, 1]), ALU.add))
                    if kind in "gu":
                        scap = bass.AP(gusT.ap().tensor, 16 * oc,
                                       [[88 * 16, 128], [1, 16], [0, 128]])
                    else:
                        scap = bass.AP(dsT.ap().tensor, 44 * oc + 4 * r,
                                       [[16 * 44, 128], [1, 4], [0, 128]])
                    vector.tensor_tensor(
                        dstv.rearrange("p (g q) -> p g q", q=128),
                        dstv.rearrange("p (g q) -> p g q", q=128),
                        scap, ALU.mult).then_inc(vdq, 1)

                def s2_unit(w_, u):
                    idx = UN * w_ + u
                    oc2, ntok = u // NSS, u % NSS
                    vector.wait_ge(sact, idx + 1)
                    if u == 0:
                        vector.wait_ge(myp, 4 * NT * w_)
                    vector.tensor_tensor(hT[:, oc2, 512 * ntok:512 * (ntok + 1)],
                                         hg[:, :], pgu[:, idx % 2, 1, :],
                                         ALU.mult).then_inc(vh2, 1)

                def s3_tile(w_, t):
                    idx = NT * w_ + t
                    sl = t % 2
                    if idx >= 2:
                        vector.wait_ge(oyd[sl], 16 * (idx // 2))
                    for n in range(4):
                        vector.wait_ge(myp, 4 * idx + n + 1)
                        inst = vector.tensor_copy(yev[:, sl, 512 * n:512 * (n + 1)],
                                                  pyp[:, n % 2, :])
                        inst.then_inc(eyp, 1)

                for r in range(NR + 1):
                    rocs = [s for s in SCHED if s[0] == r] if r < NR else []
                    if r < NR:
                        vector.wait_ge(mtq, 32 * r)
                    ns2 = ns3 = 0
                    for i, (r_, kind, oc, j, li, wb) in enumerate(rocs):
                        deq_oc(r, kind, oc, j, li)
                        if r >= 1:
                            while ns2 < min(UN, ((i + 1) * UN) // 16):
                                s2_unit(r - 1, ns2)
                                ns2 += 1
                            if i >= 16:
                                while ns3 < ((i - 15) * NT) // 8:
                                    s3_tile(r - 1, ns3)
                                    ns3 += 1
                    if r >= 1:
                        while ns2 < UN:
                            s2_unit(r - 1, ns2)
                            ns2 += 1
                        while ns3 < NT:
                            s3_tile(r - 1, ns3)
                            ns3 += 1

            @blk.scalar
            def _(scalar):
                gw2 = 0
                for r in range(NR + 1):
                    if r >= 1:
                        w_ = r - 1
                        for u in range(UN):
                            idx = UN * w_ + u
                            scalar.wait_ge(mgu, idx + 1)
                            if idx >= 1:
                                scalar.wait_ge(vh2, idx)
                            scalar.activation(hg[:, :], pgu[:, idx % 2, 0, :],
                                              AF.Silu,
                                              bias=0.0, scale=1.0).then_inc(sact, 1)
                    if r < NR:
                        if r >= 1:
                            scalar.wait_ge(mgu, UN * r)
                            scalar.wait_ge(myp, 4 * NT * r)
                        for (r_, kind, oc, j, li, wb) in [s for s in SCHED if s[0] == r]:
                            nwv = 2 if kind in "gu" else 1
                            for wv in range(nwv):
                                cnt = 8 if kind in "gu" else 4
                                scalar.wait_ge(mtq, gw2 + 1)
                                if kind == "g":
                                    dst = gTg[:, 8 * wv:8 * wv + cnt, 128 * j:128 * (j + 1)]
                                elif kind == "u":
                                    dst = gTu[:, 8 * wv:8 * wv + cnt, 128 * j:128 * (j + 1)]
                                else:
                                    dst = wdT[:, 0:4, 128 * j:128 * (j + 1)]
                                scalar.copy(dst, psT[:, gw2 % 2, 0:cnt, :]).then_inc(etq, 1)
                                gw2 += 1

            @blk.tensor
            def _(tensor):
                tensor.wait_ge(ldx, 16 * 5)
                gw = 0
                for r in range(NR + 1):
                    if r >= 1:
                        w_ = r - 1
                        tensor.wait_ge(etq, WAVES_PER_ROUND * r)
                        for u in range(UN):
                            idx = UN * w_ + u
                            oc2, ntok = u // NSS, u % NSS
                            if idx >= 2:
                                tensor.wait_ge(vh2, idx - 1)
                            for k in range(NF):
                                tensor.matmul(pgu[:, idx % 2, 0, :],
                                              gTg[:, k, 128 * oc2:128 * (oc2 + 1)],
                                              xnT[:, k, 512 * ntok:512 * (ntok + 1)],
                                              start=(k == 0), stop=(k == NF - 1))
                            for k in range(NF):
                                inst = tensor.matmul(pgu[:, idx % 2, 1, :],
                                                     gTu[:, k, 128 * oc2:128 * (oc2 + 1)],
                                                     xnT[:, k, 512 * ntok:512 * (ntok + 1)],
                                                     start=(k == 0), stop=(k == NF - 1))
                            inst.then_inc(mgu, 1)
                        tensor.wait_ge(vh2, UN * r)
                        for t in range(NT):
                            idx = NT * w_ + t
                            for n in range(4):
                                if 4 * idx + n >= 2:
                                    tensor.wait_ge(eyp, 4 * idx + n - 1)
                                for k in range(4):
                                    inst = tensor.matmul(pyp[:, n % 2, :],
                                                         hT[:, k, 128 * t:128 * (t + 1)],
                                                         wdT[:, k, 512 * n:512 * (n + 1)],
                                                         start=(k == 0), stop=(k == 3))
                                inst.then_inc(myp, 1)
                    if r < NR:
                        for (r_, kind, oc, j, li, wb) in [s for s in SCHED if s[0] == r]:
                            nwv = 2 if kind in "gu" else 1
                            tensor.wait_ge(vdq, li + 1)
                            srcv = (val2g[:, li % 24, :] if kind in "gu"
                                    else val2d[:, j, :])
                            for wv in range(nwv):
                                cnt = 8 if kind in "gu" else 4
                                if gw >= 2:
                                    tensor.wait_ge(etq, gw - 1)
                                for i2 in range(cnt):
                                    inst = tensor.transpose(
                                        psT[:, gw % 2, i2, :],
                                        srcv[:, 128 * (8 * wv + i2):128 * (8 * wv + i2 + 1)],
                                        identS[:, :])
                                inst.then_inc(mtq, 1)
                                gw += 1

            @blk.gpsimd
            def _(g):
                for r in range(NR):
                    for t in range(NT):
                        idx = NT * r + t
                        sl = t % 2
                        g.wait_ge(eyp, 4 * (idx + 1))
                        g.dma_start(out=T["ydram"][128 * t:128 * (t + 1), :],
                                    in_=yev[:, sl, :],
                                    accum_op=(ALU.add if r > 0 else ALU.bypass)
                                    ).then_inc(oyd[sl], 16)
                g.wait_ge(oyd[0], 16 * NR * NT // 2)
                g.wait_ge(oyd[1], 16 * NR * NT // 2)

        mctx.close()

        with ExitStack() as ctx:
            e = ctx.enter_context
            yf = e(nc.sbuf_tensor(tag + "_yf", [128, 2, D], F32))
            y0f = e(nc.sbuf_tensor(tag + "_y0f", [128, 2, D], F32))
            adf = e(nc.sbuf_tensor(tag + "_adf", [128, 2, D], F32))
            amf = e(nc.sbuf_tensor(tag + "_amf", [128, 2, D], F32))
            sqs = e(nc.sbuf_tensor(tag + "_sqs", [128, D], BF16))
            stt = e(nc.sbuf_tensor(tag + "_stt", [128, 4], F32))
            ly = [asem(nc, tag + "_ly%d" % i) for i in range(2)]
            l0 = [asem(nc, tag + "_l0%d" % i) for i in range(2)]
            l1_ = [asem(nc, tag + "_l1%d" % i) for i in range(2)]
            l2_ = [asem(nc, tag + "_l2%d" % i) for i in range(2)]
            ssq = asem(nc, tag + "_ssq")
            sln = asem(nc, tag + "_sln")
            srs = asem(nc, tag + "_srs")
            vfin = asem(nc, tag + "_vfin")
            vch2 = asem(nc, tag + "_vch2")
            oout = [adsem(nc, tag + "_oo%d" % i) for i in range(2)]
            blk = e(nc.Block())

            @blk.sync
            def _(sync):
                for t in range(NT):
                    sl = t % 2
                    if t >= 2:
                        sync.wait_ge(vfin, t - 1)
                        sync.wait_ge(oout[sl], 16 * (t // 2))
                    sync.dma_start(out=yf[:, sl, :],
                                   in_=T["ydram"][128 * t:128 * (t + 1), :]).then_inc(ly[sl], 16)
                    sync.dma_start(out=y0f[:, sl, :],
                                   in_=T["y0"][128 * t:128 * (t + 1), :]).then_inc(l0[sl], 16)
                    sync.dma_start(out=adf[:, sl, :],
                                   in_=T["adot"][128 * t:128 * (t + 1), :]).then_inc(l1_[sl], 16)
                    sync.dma_start(out=amf[:, sl, :],
                                   in_=T["amem"][128 * t:128 * (t + 1), :]).then_inc(l2_[sl], 16)

            @blk.scalar
            def _(scalar):
                for t in range(NT):
                    sl = t % 2
                    scalar.wait_ge(ly[sl], 16 * (t // 2 + 1))
                    if t >= 1:
                        scalar.wait_ge(vch2, 3 * (t - 1) + 1)
                    scalar.activation(sqs[:, :], yf[:, sl, :], AF.Square,
                                      accum_out=stt[:, 0:1]).then_inc(ssq, 1)
                    scalar.wait_ge(ssq, t + 1)
                    scalar.activation(stt[:, 1:2], stt[:, 0:1], AF.Ln,
                                      bias=np.float32(1e-6),
                                      scale=np.float32(1.0 / D)).then_inc(sln, 1)
                    scalar.wait_ge(sln, t + 1)
                    scalar.activation(stt[:, 2:3], stt[:, 1:2], AF.Exp, bias=0.0,
                                      scale=np.float32(-0.5)).then_inc(srs, 1)

            @blk.vector
            def _(vector):
                nv = [0]
                def step(inst):
                    nv[0] += 1
                    inst.then_inc(vch2, 1)
                    vector.wait_ge(vch2, nv[0])
                for t in range(NT):
                    sl = t % 2
                    vector.wait_ge(srs, t + 1)
                    vector.wait_ge(l0[sl], 16 * (t // 2 + 1))
                    vector.wait_ge(l1_[sl], 16 * (t // 2 + 1))
                    vector.wait_ge(l2_[sl], 16 * (t // 2 + 1))
                    if t >= 2:
                        vector.wait_ge(oout[sl], 16 * (t // 2))
                    step(vector.tensor_scalar(yf[:, sl, :], yf[:, sl, :],
                                              stt[:, 2:3], None, ALU.mult))
                    step(vector.tensor_tensor(yf[:, sl, :], yf[:, sl, :],
                                              y0f[:, sl, :], ALU.add))
                    step(vector.tensor_tensor(yf[:, sl, :], yf[:, sl, :],
                                              adf[:, sl, :], ALU.add))
                    vector.tensor_tensor(yf[:, sl, :], yf[:, sl, :],
                                         amf[:, sl, :], ALU.add).then_inc(vfin, 1)

            @blk.gpsimd
            def _(g):
                for t in range(NT):
                    sl = t % 2
                    g.wait_ge(vfin, t + 1)
                    g.dma_start(out=T["out"][128 * t:128 * (t + 1), :],
                                in_=yf[:, sl, :]).then_inc(oout[sl], 16)
                g.wait_ge(oout[0], 16 * NT // 2)
                g.wait_ge(oout[1], 16 * NT // 2)
    return nc



def _consts_full():
    c = host_consts()
    inv = 1.0 / (10000.0 ** (np.arange(0, 64, 2, dtype=np.float32) / 64.0))
    ang = np.arange(S, dtype=np.float32)[:, None] * inv[None, :]
    ang = np.concatenate([ang, ang], axis=-1)
    c["cosN"] = np.cos(ang).astype(np.float32)
    c["sinN"] = np.sin(ang).astype(np.float32)
    c["cos2b"] = c["cos2"].astype(bfdt)
    c["sin2b"] = c["sin2"].astype(bfdt)
    c["ones1f"] = np.ones((1, 128), dtype=np.float32)
    return c


def _di(nc, T, ins, name, arr, dt, kind="ExternalInput"):
    T[name] = nc.dram_tensor(name, list(arr.shape), dt, kind=kind)
    ins[name] = arr


def build_prep(c):
    nc = bass.Bass("TRN2", target_bir_lowering=False)
    T, ins = {}, {}
    x_d = nc.dram_tensor("x", [S, D], F32, kind="ExternalInput")
    xnT_d = nc.dram_tensor("xnT", [D, S], BF16, kind="ExternalOutput")
    _di(nc, T, ins, "ident", c["ident"], BF16)
    prime_consts(nc)
    emit_prep(nc, x_d, xnT_d, T["ident"])
    return nc, ins


def build_attn0(c, w):
    nc = bass.Bass("TRN2", target_bir_lowering=False)
    T, ins = {}, {}
    _di(nc, T, ins, "xnT", None, BF16) if False else None
    T["xnT"] = nc.dram_tensor("xnT", [D, S], BF16, kind="ExternalInput")
    _di(nc, T, ins, "w_dq", w["w_dq"], F32)
    _di(nc, T, ins, "qnsT", w["qnsT"], F32)
    _di(nc, T, ins, "w_uq", w["w_uq"], F32)
    _di(nc, T, ins, "w_dkv", w["w_dkv"], F32)
    _di(nc, T, ins, "kvnsT", w["kvnsT"], F32)
    _di(nc, T, ins, "w_ukv", w["w_ukv"], F32)
    _di(nc, T, ins, "w_kpe", w["w_kpe"], F32)
    _di(nc, T, ins, "w_o", w["w_o"], F32)
    for k in ["ident", "r2t", "tril", "cosN", "sinN"]:
        dt = BF16 if c[k].dtype == bfdt else F32
        _di(nc, T, ins, k, c[k], dt)
    _di(nc, T, ins, "cos2", c["cos2b"], BF16)
    _di(nc, T, ins, "sin2", c["sin2b"], BF16)
    T["attn_oT"] = nc.dram_tensor("attn_oT", [D, S], BF16)
    T["attn0"] = nc.dram_tensor("attn0", [S, D], F32, kind="ExternalOutput")
    prime_consts(nc)
    emit_attn0(nc, T)
    return nc, ins


def build_ffn0(c, w):
    nc = bass.Bass("TRN2", target_bir_lowering=False)
    T, ins = {}, {}
    T["xnT"] = nc.dram_tensor("xnT", [D, S], BF16, kind="ExternalInput")
    T["x"] = nc.dram_tensor("x", [S, D], F32, kind="ExternalInput")
    T["attn0"] = nc.dram_tensor("attn0", [S, D], F32, kind="ExternalInput")
    _di(nc, T, ins, "gate_wT", w["gate_wT"], F32)
    _di(nc, T, ins, "sgwT", w["sgwT"], F32)
    _di(nc, T, ins, "w_gate", w["w_gate"], F32)
    _di(nc, T, ins, "w_up", w["w_up"], F32)
    _di(nc, T, ins, "w_down", w["w_down"], F32)
    _di(nc, T, ins, "sw_gate", w["sw_gate"], F32)
    _di(nc, T, ins, "sw_up", w["sw_up"], F32)
    _di(nc, T, ins, "sw_down", w["sw_down"], F32)
    _di(nc, T, ins, "ident", c["ident"], BF16)
    T["y0"] = nc.dram_tensor("y0", [S, D], F32, kind="ExternalOutput")
    prime_consts(nc)
    emit_ffn0(nc, T)
    return nc, ins


def build_attn1(c, w):
    nc = bass.Bass("TRN2", target_bir_lowering=False)
    T, ins = {}, {}
    T["xnT"] = nc.dram_tensor("xnT", [D, S], BF16, kind="ExternalInput")
    _di(nc, T, ins, "WqT", w["WqT"], F32)
    _di(nc, T, ins, "WkT", w["WkT"], F32)
    _di(nc, T, ins, "WvT", w["WvT"], F32)
    _di(nc, T, ins, "memory", w["memory"], F32)
    _di(nc, T, ins, "mnT", w["mnT"], F32)
    _di(nc, T, ins, "mw", w["mw"], F32)
    _di(nc, T, ins, "ident", c["ident"], BF16)
    _di(nc, T, ins, "ones1f", c["ones1f"], F32)
    T["adot"] = nc.dram_tensor("adot", [S, D], F32, kind="ExternalOutput")
    T["amem"] = nc.dram_tensor("amem", [S, D], F32, kind="ExternalOutput")
    prime_consts(nc)
    emit_attn1(nc, T)
    return nc, ins


def build_ffn1(c, w):
    nc = bass.Bass("TRN2", target_bir_lowering=False)
    T, ins = {}, {}
    T["xnT"] = nc.dram_tensor("xnT", [D, S], BF16, kind="ExternalInput")
    _di(nc, T, ins, "gup", w["gup"], I32)
    _di(nc, T, ins, "gusT", w["gusT"], F16)
    _di(nc, T, ins, "dwp", w["dwp"], I32)
    _di(nc, T, ins, "dsT", w["dsT"], F16)
    _di(nc, T, ins, "ident", c["ident"], BF16)
    T["y0"] = nc.dram_tensor("y0", [S, D], F32, kind="ExternalInput")
    T["adot"] = nc.dram_tensor("adot", [S, D], F32, kind="ExternalInput")
    T["amem"] = nc.dram_tensor("amem", [S, D], F32, kind="ExternalInput")
    T["ydram"] = nc.dram_tensor("ydram", [S, D], F32)
    T["out"] = nc.dram_tensor("out", [S, D], F32, kind="ExternalOutput")
    prime_consts(nc)
    emit_ffn1(nc, T)
    return nc, ins


def kernel(L0_attn_x, L0_attn_w_dq, L0_attn_q_norm_scale, L0_attn_w_uq,
           L0_attn_w_dkv, L0_attn_kv_norm_scale, L0_attn_w_ukv, L0_attn_w_kpe,
           L0_attn_w_o, L0_ffn_gate_w, L0_ffn_w_gate, L0_ffn_w_up,
           L0_ffn_w_down, L0_ffn_sw_gate, L0_ffn_sw_up, L0_ffn_sw_down,
           L0_ffn_shared_gate_w, L1_attn_Wq, L1_attn_Wk, L1_attn_Wv,
           L1_attn_memory, L1_attn_memory_norm, L1_attn_memory_weight,
           L1_ffn_gate_up_packed, L1_ffn_gate_up_scales, L1_ffn_down_packed,
           L1_ffn_down_scales):
    NC = 8
    c = _consts_full()
    ct = np.ascontiguousarray
    w = {
        "w_dq": ct(L0_attn_w_dq), "qnsT": ct(L0_attn_q_norm_scale.reshape(12, 128).T),
        "w_uq": ct(L0_attn_w_uq), "w_dkv": ct(L0_attn_w_dkv),
        "kvnsT": ct(L0_attn_kv_norm_scale.reshape(4, 128).T),
        "w_ukv": ct(L0_attn_w_ukv), "w_kpe": ct(L0_attn_w_kpe),
        "w_o": ct(L0_attn_w_o),
        "gate_wT": ct(L0_ffn_gate_w.T), "sgwT": ct(L0_ffn_shared_gate_w.reshape(16, 128).T),
        "w_gate": ct(L0_ffn_w_gate), "w_up": ct(L0_ffn_w_up),
        "w_down": ct(L0_ffn_w_down), "sw_gate": ct(L0_ffn_sw_gate),
        "sw_up": ct(L0_ffn_sw_up), "sw_down": ct(L0_ffn_sw_down),
        "WqT": ct(L1_attn_Wq.T), "WkT": ct(L1_attn_Wk.T), "WvT": ct(L1_attn_Wv.T),
        "memory": ct(L1_attn_memory), "mnT": ct(L1_attn_memory_norm.reshape(4, 128).T),
        "mw": ct(L1_attn_memory_weight.reshape(1, 1)),
        "gup": ct(L1_ffn_gate_up_packed), "gusT": ct(L1_ffn_gate_up_scales.T),
        "dwp": ct(L1_ffn_down_packed), "dsT": ct(L1_ffn_down_scales.T),
    }
    xs = [np.ascontiguousarray(L0_attn_x[b]) for b in range(NC)]
    cores = list(range(NC))
    if FUSED:
        return kernel_fused(w, xs)

    def run(build, per_core_extra, outs):
        nc, shared = build
        maps = []
        for b in range(NC):
            m = dict(shared)
            m.update(per_core_extra[b])
            maps.append(m)
        res = run_bass_kernel_spmd(nc, maps, core_ids=cores)
        return [{k: r[k] for k in outs} for r in res.results]

    r1 = run(build_prep(c), [{"x": xs[b]} for b in range(NC)], ["xnT"])
    r2 = run(build_attn0(c, w), [{"xnT": r1[b]["xnT"]} for b in range(NC)], ["attn0"])
    r3 = run(build_ffn0(c, w),
             [{"xnT": r1[b]["xnT"], "x": xs[b], "attn0": r2[b]["attn0"]}
              for b in range(NC)], ["y0"])
    r4 = run(build_prep(c), [{"x": r3[b]["y0"]} for b in range(NC)], ["xnT"])
    r5 = run(build_attn1(c, w), [{"xnT": r4[b]["xnT"]} for b in range(NC)],
             ["adot", "amem"])
    r6 = run(build_ffn1(c, w),
             [{"xnT": r4[b]["xnT"], "y0": r3[b]["y0"], "adot": r5[b]["adot"],
               "amem": r5[b]["amem"]} for b in range(NC)], ["out"])
    return np.stack([r6[b]["out"] for b in range(NC)], axis=0)



def build_fused(c, w, upto=6):
    nc = bass.Bass("TRN2", target_bir_lowering=False)
    T, ins = {}, {}
    x_d = nc.dram_tensor("x", [S, D], F32, kind="ExternalInput")
    out_d = nc.dram_tensor("out", [S, D], F32, kind="ExternalOutput")
    for k in ["w_dq", "qnsT", "w_uq", "w_dkv", "kvnsT", "w_ukv", "w_kpe",
              "w_o", "gate_wT", "sgwT", "w_gate", "w_up", "w_down", "sw_gate",
              "sw_up", "sw_down", "WqT", "WkT", "WvT", "memory", "mnT", "mw"]:
        _di(nc, T, ins, k, w[k], F32)
    _di(nc, T, ins, "gup", w["gup"], I32)
    _di(nc, T, ins, "gusT", w["gusT"], F16)
    _di(nc, T, ins, "dwp", w["dwp"], I32)
    _di(nc, T, ins, "dsT", w["dsT"], F16)
    for k in ["ident", "r2t"]:
        _di(nc, T, ins, k, c[k], BF16)
    _di(nc, T, ins, "cos2", c["cos2b"], BF16)
    _di(nc, T, ins, "sin2", c["sin2b"], BF16)
    for k in ["tril", "cosN", "sinN", "ones1f"]:
        _di(nc, T, ins, k, c[k], F32)
    xn0T = nc.dram_tensor("xn0T", [D, S], BF16)
    xn1T = nc.dram_tensor("xn1T", [D, S], BF16)
    T["attn_oT"] = nc.dram_tensor("attn_oT", [D, S], BF16)
    T["attn0"] = nc.dram_tensor("attn0", [S, D], F32)
    T["x"] = x_d
    T["y0"] = nc.dram_tensor("y0", [S, D], F32)
    T["adot"] = nc.dram_tensor("adot", [S, D], F32)
    T["amem"] = nc.dram_tensor("amem", [S, D], F32)
    T["ydram"] = nc.dram_tensor("ydram", [S, D], F32)
    T["out"] = out_d

    prime_consts(nc)
    with nc.named_scope("prep0"):
        emit_prep(nc, x_d, xn0T, T["ident"], tag="p0")
    T["xnT"] = xn0T
    if upto >= 2:
        with nc.named_scope("attn0"):
            emit_attn0(nc, T)
    if upto >= 3:
        with nc.named_scope("ffn0"):
            emit_ffn0(nc, T)
    if upto >= 4:
        with nc.named_scope("prep1"):
            emit_prep(nc, T["y0"], xn1T, T["ident"], tag="p1")
        T["xnT"] = xn1T
    if upto >= 5:
        with nc.named_scope("attn1"):
            emit_attn1(nc, T)
    if upto >= 6:
        with nc.named_scope("ffn1"):
            emit_ffn1(nc, T)
    return nc, ins


def kernel_fused(inputs_w, xs):
    c = _consts_full()
    nc, shared = build_fused(c, inputs_w)
    maps = []
    for b in range(8):
        m = dict(shared)
        m["x"] = xs[b]
        maps.append(m)
    res = run_bass_kernel_spmd(nc, maps, core_ids=list(range(8)))
    return np.stack([r["out"] for r in res.results], axis=0)
